# version 04
# baseline (speedup 1.0000x reference)
import numpy as np
import ml_dtypes
from contextlib import ExitStack, contextmanager

import concourse.bass as bass
import concourse.mybir as mybir
from concourse.bass_utils import run_bass_kernel_spmd

F32 = mybir.dt.float32
BF16 = mybir.dt.bfloat16
I32 = mybir.dt.int32
AF = mybir.ActivationFunctionType
ALU = mybir.AluOpType
AX = mybir.AxisListType
BFNP = ml_dtypes.bfloat16

NCORE = 8
B, T, D = 4, 4096, 1024
TOWN = T // 2
QC = 512
NQC = TOWN // QC
D8 = D // 128
EPS = 1e-5
NEG_BIG = -1.0e30
DBG_L3F = False


class Op:
    __slots__ = ("stream", "emit", "deps", "kind", "sig", "needs_inc", "tag")

    def __init__(self, stream, emit, deps, kind, tag):
        self.stream = stream
        self.emit = emit
        self.deps = deps
        self.kind = kind
        self.sig = None
        self.needs_inc = False
        self.tag = tag


class Tile:
    __slots__ = ("ap", "name", "writer", "readers", "dmasem", "scope_id")

    def __init__(self, ap, name):
        self.ap = ap
        self.name = name
        self.writer = None
        self.readers = []
        self.dmasem = None
        self.scope_id = None

    def __getitem__(self, idx):
        return self.ap[idx]


class Prog:

    STREAMS = ("pe", "act", "dve", "gp", "sync")
    DMA_SEM_POOL = 40

    def __init__(self, nc):
        self.nc = nc
        self._stacks = [ExitStack()]
        self.ops = []
        self.esem = {}
        self._semcount = 0
        self._scope_id = 0
        self._scope_last = {}
        self._scope_dmatiles = set()
        self._dma_pool = []
        self._dma_pool_idx = 0
        self._sem_fills = {}
        self._sem_scope = {}
        self.n_waits = 0
        self.finalized = False

    def _sem(self, name=None):
        self._semcount += 1
        return self._stacks[0].enter_context(
            self.nc.semaphore(name or f"s{self._semcount}"))

    def _dma_sem_for(self, tile):
        if not self._dma_pool:
            self._dma_pool = [self._sem(f"dpool{i}") for i in range(self.DMA_SEM_POOL)]
            for s in self._dma_pool:
                self._sem_fills[id(s)] = 0
        for _ in range(self.DMA_SEM_POOL):
            s = self._dma_pool[self._dma_pool_idx]
            self._dma_pool_idx = (self._dma_pool_idx + 1) % self.DMA_SEM_POOL
            if self._sem_scope.get(id(s)) != self._scope_id:
                self._sem_scope[id(s)] = self._scope_id
                return s
        raise RuntimeError("dma sem pool exhausted within one scope")

    def sbuf(self, name, shape, dtype):
        h = self._stacks[-1].enter_context(
            self.nc.sbuf_tensor(f"{name}_{self._scope_id}", list(shape), dtype))
        t = Tile(h[:], name)
        t.scope_id = self._scope_id
        return t

    def psum(self, name, shape, dtype=F32):
        h = self._stacks[-1].enter_context(
            self.nc.psum_tensor(f"{name}_{self._scope_id}", list(shape), dtype))
        t = Tile(h[:], name)
        t.scope_id = self._scope_id
        return t

    def dram_tile(self, ap, name):
        t = Tile(ap, name)
        t.scope_id = -1
        return t

    @contextmanager
    def scope(self):
        self._stacks.append(ExitStack())
        prev_last, self._scope_last = self._scope_last, {}
        prev_dt, self._scope_dmatiles = self._scope_dmatiles, set()
        self._scope_id += 1
        my_scope = self._scope_id
        try:
            yield
        finally:
            fence_ops = self.fence()
            self._stacks.pop().close()
            self._scope_last = prev_last
            for s, o in zip(self.STREAMS, fence_ops):
                self._scope_last[s] = o
            self._scope_dmatiles = prev_dt
            for sid, sc in list(self._sem_scope.items()):
                if sc == my_scope:
                    del self._sem_scope[sid]

    def fence(self):
        lasts = [o for o in self._scope_last.values() if o is not None]
        dma_ops = []
        for t in self._scope_dmatiles:
            if t.writer is not None and t.writer.kind == "dma":
                dma_ops.append(t.writer)
        fence_ops = []
        for s in self.STREAMS:
            deps = [o for o in lasts if o.stream != s] + dma_ops
            o = Op(s, lambda eng: eng.nop(), deps, "compute", "fence")
            self.ops.append(o)
            fence_ops.append(o)
        for s, o in zip(self.STREAMS, fence_ops):
            self._scope_last[s] = o
        return fence_ops

    def op(self, stream, emit, reads=(), writes=(), kind="compute", tag=""):
        deps = []
        for t in reads:
            if t.writer is not None:
                deps.append(t.writer)
        for t in writes:
            if t.writer is not None:
                deps.append(t.writer)
            deps.extend(t.readers)
        o = Op(stream, emit, deps, kind, tag)
        self.ops.append(o)
        self._scope_last[stream] = o
        for t in reads:
            if not t.readers or t.readers[-1] is not o:
                t.readers.append(o)
        for t in writes:
            t.writer = o
            t.readers = []
        return o

    def dma(self, queue, out_ap, in_ap, reads=(), writes=(), tag="dma", **kw):
        assert len(writes) >= 1
        dst = writes[0]
        if dst.dmasem is None or self._sem_scope.get(id(dst.dmasem)) not in (
                self._scope_id, -1):
            if dst.scope_id == -1:
                dst.dmasem = self._sem()
                self._sem_fills[id(dst.dmasem)] = 0
                self._sem_scope[id(dst.dmasem)] = -1
            else:
                dst.dmasem = self._dma_sem_for(dst)
        o = self.op(queue, lambda eng: eng.dma_start(out=out_ap, in_=in_ap, **kw),
                    reads=reads, writes=writes, kind="dma", tag=tag)
        self._sem_fills[id(dst.dmasem)] += 16
        o.sig = (dst.dmasem, self._sem_fills[id(dst.dmasem)])
        self._scope_dmatiles.add(dst)
        return o

    def coll(self, kind, aluop, groups, out_ap, in_ap, reads=(), writes=(),
             tag="coll"):
        assert len(writes) >= 1
        dst = writes[0]
        if dst.dmasem is None:
            dst.dmasem = self._sem()
            self._sem_fills[id(dst.dmasem)] = 0
            self._sem_scope[id(dst.dmasem)] = -1
        o = self.op("gp", lambda eng: eng.collective_compute(
            kind, aluop, groups, [in_ap], [out_ap]),
            reads=reads, writes=writes, kind="dma", tag=tag)
        self._sem_fills[id(dst.dmasem)] += 16
        o.sig = (dst.dmasem, self._sem_fills[id(dst.dmasem)])
        self._scope_dmatiles.add(dst)
        return o

    def finalize(self):
        assert not self.finalized
        self.finalized = True
        nc = self.nc
        for o in self.ops:
            for d in o.deps:
                if d.kind == "dma":
                    continue
                if d.stream == o.stream and d.stream == "pe":
                    continue
                d.needs_inc = True
        for s in self.STREAMS:
            self.esem[s] = self._sem(f"e_{s}")
        counts = {s: 0 for s in self.STREAMS}
        for o in self.ops:
            if o.kind == "compute" and o.needs_inc:
                counts[o.stream] += 1
                o.sig = (self.esem[o.stream], counts[o.stream])
        per_stream = {s: [o for o in self.ops if o.stream == s] for s in self.STREAMS}

        prog = self

        def make_body(stream):
            ops = per_stream[stream]

            def body(eng):
                wm = {}
                for o in ops:
                    needed = {}
                    for d in o.deps:
                        if d.sig is None:
                            continue
                        if (d.stream == o.stream and d.stream == "pe"
                                and d.kind != "dma"):
                            continue
                        s, v = d.sig
                        if needed.get(id(s), (None, 0))[1] < v:
                            needed[id(s)] = (s, v)
                    for s, v in needed.values():
                        if wm.get(id(s), 0) >= v:
                            continue
                        eng.wait_ge(s, v)
                        wm[id(s)] = v
                        prog.n_waits += 1
                    inst = o.emit(eng)
                    if o.kind == "dma":
                        inst.then_inc(o.sig[0], 16)
                    elif o.needs_inc:
                        inst.then_inc(o.sig[0], 1)
            return body

        with nc.Block() as blk:
            blk.tensor(make_body("pe"))
            blk.scalar(make_body("act"))
            blk.vector(make_body("dve"))
            blk.gpsimd(make_body("gp"))
            blk.sync(make_body("sync"))
        self._stacks[0].close()

    def mm(self, ptile, out_ap, lhsT_ap, rhs_ap, start, stop, reads=(), tag="mm"):
        return self.op("pe", lambda eng: eng.matmul(out_ap, lhsT_ap, rhs_ap,
                                                    start=start, stop=stop,
                                                    skip_group_check=True),
                       reads=reads, writes=(ptile,), tag=tag)

    def cp(self, stream, dst, dst_ap, src_ap, reads=(), tag="cp"):
        f = {"dve": "vector", "gp": "gpsimd"}[stream]
        return self.op(stream,
                       lambda eng: eng.tensor_copy(out=dst_ap, in_=src_ap),
                       reads=reads, writes=(dst,), tag=tag)

    def act(self, dst, dst_ap, src_ap, func, bias=0.0, scale=1.0, reads=(),
            tag="act"):
        return self.op("act",
                       lambda eng: eng.activation(out=dst_ap, in_=src_ap, func=func,
                                                  bias=bias, scale=scale),
                       reads=reads, writes=(dst,), tag=tag)

    def tt(self, stream, dst, dst_ap, a_ap, b_ap, aop, reads=(), tag="tt"):
        return self.op(stream,
                       lambda eng: eng.tensor_tensor(out=dst_ap, in0=a_ap, in1=b_ap,
                                                     op=aop),
                       reads=reads, writes=(dst,), tag=tag)

    def ts(self, stream, dst, dst_ap, src_ap, s1, op1, s2=None, op2=None,
           reads=(), tag="ts"):
        if s2 is None:
            return self.op(stream,
                           lambda eng: eng.tensor_single_scalar(out=dst_ap,
                                                                in_=src_ap,
                                                                scalar=s1, op=op1),
                           reads=reads, writes=(dst,), tag=tag)
        return self.op(stream,
                       lambda eng: eng.tensor_scalar(out=dst_ap, in0=src_ap,
                                                     scalar1=s1, op0=op1,
                                                     scalar2=s2, op1=op2),
                       reads=reads, writes=(dst,), tag=tag)

    def stt(self, stream, dst, dst_ap, in0_ap, scalar, in1_ap, op0, op1,
            reads=(), tag="stt"):
        return self.op(stream,
                       lambda eng: eng.scalar_tensor_tensor(out=dst_ap, in0=in0_ap,
                                                            scalar=scalar, in1=in1_ap,
                                                            op0=op0, op1=op1),
                       reads=reads, writes=(dst,), tag=tag)

    def red(self, dst, dst_ap, src_ap, axis, rop, reads=(), tag="red"):
        return self.op("dve",
                       lambda eng: eng.tensor_reduce(out=dst_ap, in_=src_ap,
                                                     axis=axis, op=rop),
                       reads=reads, writes=(dst,), tag=tag)

    def memset(self, stream, dst, dst_ap, val, tag="memset"):
        return self.op(stream, lambda eng: eng.memset(dst_ap, val),
                       writes=(dst,), tag=tag)

    def recip(self, dst, dst_ap, src_ap, reads=(), tag="recip"):
        self.op("act", lambda eng: eng.activation(out=dst_ap, in_=src_ap,
                                                  func=AF.Ln),
                reads=reads, writes=(dst,), tag=tag + "_ln")
        return self.op("act", lambda eng: eng.activation(out=dst_ap, in_=dst_ap,
                                                         func=AF.Exp,
                                                         scale=-1.0),
                       writes=(dst,), tag=tag + "_ex")

    def sqrt_pos(self, dst, dst_ap, src_ap, reads=(), tag="sqrtp"):
        self.op("act", lambda eng: eng.activation(out=dst_ap, in_=src_ap,
                                                  func=AF.Ln),
                reads=reads, writes=(dst,), tag=tag + "_ln")
        return self.op("act", lambda eng: eng.activation(out=dst_ap, in_=dst_ap,
                                                         func=AF.Exp,
                                                         scale=0.5),
                       writes=(dst,), tag=tag + "_ex")

    def rsqrt_pos(self, dst, dst_ap, src_ap, reads=(), tag="rsqp"):
        self.op("act", lambda eng: eng.activation(out=dst_ap, in_=src_ap,
                                                  func=AF.Ln),
                reads=reads, writes=(dst,), tag=tag + "_ln")
        return self.op("act", lambda eng: eng.activation(out=dst_ap, in_=dst_ap,
                                                         func=AF.Exp,
                                                         scale=-0.5),
                       writes=(dst,), tag=tag + "_ex")



def _c(a):
    return np.ascontiguousarray(a)


def _2d(v):
    C = v.shape[0] // 128
    return _c(np.asarray(v).reshape(C, 128).T)


def _pad_cols(a, n):
    out = np.zeros(a.shape[:-1] + (n,), a.dtype)
    out[..., :a.shape[-1]] = a
    return out


def _pad_rows2(a, n):
    out = np.zeros((a.shape[0], n, a.shape[2]), a.dtype)
    out[:, :a.shape[1], :] = a
    return out


def host_prep(inp):
    S = {}
    P = [dict() for _ in range(NCORE)]
    f32 = np.float32

    perm = np.concatenate([np.arange(0, D, 2), np.arange(1, D, 2)])
    S["l0a_wqT"] = _c(np.asarray(inp["L0_attn_Wq"], f32).T[:, perm])
    S["l0a_wkT"] = _c(np.asarray(inp["L0_attn_Wk"], f32).T[:, perm])
    S["l0a_wvT"] = _c(np.asarray(inp["L0_attn_Wv"], f32).T)
    half = D // 2
    i = np.arange(half, dtype=np.float64)
    theta = 10000.0 ** (-2.0 * i / D)
    m = np.arange(T, dtype=np.float64)[:, None]
    ang = m * theta[None, :]
    S["l0a_cosk"] = _c(np.cos(ang).T.astype(f32))
    S["l0a_sink"] = _c(np.sin(ang).T.astype(f32))
    tkl = np.arange(128)[None, :, None]
    tql = np.arange(512)[None, None, :]
    kb = np.arange(32)[:, None, None]
    for c in range(NCORE):
        h = c % 2
        Q0 = TOWN * h
        m0 = np.zeros((NQC, 32, 128, 512), BFNP)
        m2 = np.zeros((NQC, 32, 128, 512), BFNP)
        for qc in range(NQC):
            gq = Q0 + qc * QC
            m0[qc] = ((kb * 128 + tkl) <= (gq + tql + 1)).astype(f32)
            m2[qc] = ((kb * 128 + tkl) <= (gq + tql)).astype(f32)
        P[c]["l0a_mask"] = m0
        P[c]["l2a_mask"] = m2
        P[c]["l0a_cosq"] = _c(S["l0a_cosk"][:, Q0:Q0 + TOWN])
        P[c]["l0a_sinq"] = _c(S["l0a_sink"][:, Q0:Q0 + TOWN])

    S["l0f_wgT"] = _c(np.asarray(inp["L0_ffn_w_gate"]).T)
    S["l0f_wuT"] = _c(np.asarray(inp["L0_ffn_w_up"]).T)
    S["l0f_wdT"] = _c(np.asarray(inp["L0_ffn_w_down"]).T)

    S["l1a_qkvT"] = _c(np.asarray(inp["L1_attn_qkv_weight"], f32).T)
    S["l1a_qkvmT"] = _c(np.asarray(inp["L1_attn_qkv_weight_mask"], f32).T)
    S["l1a_owT"] = _c(np.asarray(inp["L1_attn_out_weight"], f32).T)
    S["l1a_owmT"] = _c(np.asarray(inp["L1_attn_out_weight_mask"], f32).T)
    S["l1a_ob"] = _2d(np.asarray(inp["L1_attn_out_bias"], f32))
    S["l1a_obm"] = _2d(np.asarray(inp["L1_attn_out_bias_mask"], f32))

    S["l1f_routerT"] = _c(np.asarray(inp["L1_ffn_router_w"], f32).T)
    S["l1f_rbias"] = _c(np.asarray(inp["L1_ffn_bias"], f32).reshape(1, 16))
    S["l1f_gate"] = _c(np.asarray(inp["L1_ffn_gate_w"], f32))
    S["l1f_up"] = _c(np.asarray(inp["L1_ffn_up_w"], f32))
    S["l1f_down"] = _c(np.asarray(inp["L1_ffn_down_w"], f32))
    S["l1f_shg"] = _c(np.asarray(inp["L1_ffn_sh_gate_w"], f32))
    S["l1f_shu"] = _c(np.asarray(inp["L1_ffn_sh_up_w"], f32))
    S["l1f_shd"] = _c(np.asarray(inp["L1_ffn_sh_down_w"], f32))
    S["iota16"] = _c(np.arange(16, dtype=f32).reshape(1, 16))
    S["eye128"] = _c(np.eye(128, dtype=f32).astype(BFNP))

    S["l2a_wq"] = _c(np.asarray(inp["L2_attn_Wq"], f32))
    S["l2a_wk"] = _c(np.asarray(inp["L2_attn_Wk"], f32))
    S["l2a_wv"] = _c(np.asarray(inp["L2_attn_Wv"], f32))
    S["l2a_wo"] = _c(np.asarray(inp["L2_attn_Wo"], f32))
    S["l2a_bo"] = _2d(np.asarray(inp["L2_attn_bo"], f32))

    S["l2f_w1T"] = _c(np.asarray(inp["L2_ffn_w1_w"], f32).T)
    S["l2f_w2T"] = _c(np.asarray(inp["L2_ffn_w2_w"], f32).T)
    S["l2f_h1T"] = _c(np.asarray(inp["L2_ffn_h1_w"], f32).T)
    S["l2f_h2T"] = _c(np.asarray(inp["L2_ffn_h2_w"], f32).T)
    S["l2f_w3T"] = _c(np.asarray(inp["L2_ffn_w3_w"], f32).T)
    S["l2f_w1s"] = _2d(np.asarray(inp["L2_ffn_w1_s"], f32))
    S["l2f_w2s"] = _2d(np.asarray(inp["L2_ffn_w2_s"], f32))
    S["l2f_h1s"] = _2d(np.asarray(inp["L2_ffn_h1_s"], f32))
    S["l2f_h2s"] = _2d(np.asarray(inp["L2_ffn_h2_s"], f32))
    S["l2f_w3s"] = _2d(np.asarray(inp["L2_ffn_w3_s"], f32))

    S["l3a_wqkvT"] = _c(np.asarray(inp["L3_attn_Wqkv"], f32).T)
    bq = np.asarray(inp["L3_attn_bqkv"], f32)
    S["l3a_bqkv"] = _2d(bq)
    S["l3a_bvrow"] = _c(bq[2 * D:3 * D].reshape(1, D))
    S["l3a_woutT"] = _c(np.asarray(inp["L3_attn_Wout"], f32).T)
    S["l3a_bout"] = _2d(np.asarray(inp["L3_attn_bout"], f32))
    S["l3a_proj"] = _c(np.asarray(inp["L3_attn_proj"], f32))
    s_ = np.arange(128)[:, None]
    t_ = np.arange(128)[None, :]
    S["l3a_tril"] = _c((s_ <= t_).astype(f32))

    S["l3f_wqkv"] = _c(np.asarray(inp["L3_ffn_wqkv"], f32))
    S["l3f_qs"] = _c(np.asarray(inp["L3_ffn_q_scale"], f32).reshape(2, 64, 1))
    S["l3f_ks"] = _c(np.asarray(inp["L3_ffn_k_scale"], f32).reshape(2, 64, 1))
    S["l3f_wo"] = _c(np.asarray(inp["L3_ffn_wo"], f32))
    S["l3f_lng"] = np.stack([_2d(np.asarray(inp["L3_ffn_ln_g"], f32)[l])
                             for l in range(2)])
    S["l3f_lnb"] = np.stack([_2d(np.asarray(inp["L3_ffn_ln_b"], f32)[l])
                             for l in range(2)])
    HID = 2730
    HPAD = 2816
    w1 = np.asarray(inp["L3_ffn_w1"], f32)
    S["l3f_w1a"] = _c(_pad_cols(w1[:, :, :HID], HPAD))
    S["l3f_w1g"] = _c(_pad_cols(w1[:, :, HID:], HPAD))
    b1 = np.asarray(inp["L3_ffn_b1"], f32)
    S["l3f_b1a"] = np.stack([_2d(_pad_cols(b1[l, :HID], HPAD)) for l in range(2)])
    S["l3f_b1g"] = np.stack([_2d(_pad_cols(b1[l, HID:], HPAD)) for l in range(2)])
    S["l3f_w2"] = _c(_pad_rows2(np.asarray(inp["L3_ffn_w2"], f32), HPAD))
    S["l3f_b2"] = np.stack([_2d(np.asarray(inp["L3_ffn_b2"], f32)[l])
                            for l in range(2)])
    DH, WIN = 64, 512
    hi = np.arange(0, DH, 2, dtype=np.float64)
    inv_freq = 1.0 / 10000.0 ** (hi / DH)
    t = np.arange(T, dtype=np.float64)
    freqs = t[:, None] * inv_freq[None, :]
    freqs = np.concatenate([freqs, freqs], axis=-1)
    base = (hi + 0.4 * DH) / (1.4 * DH)
    power = (t - T // 2) / float(WIN // 2)
    scale = base[None, :] ** power[:, None]
    scale = np.concatenate([scale, scale], axis=-1)
    tw0 = (np.arange(T) // 512) * 512
    sW0 = scale[tw0]
    powp = (tw0 + 512 - T // 2) / float(WIN // 2)
    sW0p = np.concatenate([base[None, :] ** powp[:, None]] * 2, axis=-1)
    S["l3f_csq"] = _c((np.cos(freqs) * scale / sW0).T.astype(f32))
    S["l3f_ssq"] = _c((np.sin(freqs) * scale / sW0).T.astype(f32))
    S["l3f_csk"] = _c((np.cos(freqs) / scale * sW0).T.astype(f32))
    S["l3f_ssk"] = _c((np.sin(freqs) / scale * sW0).T.astype(f32))
    S["l3f_cskp"] = _c((np.cos(freqs) / scale * sW0p).T.astype(f32))
    S["l3f_sskp"] = _c((np.sin(freqs) / scale * sW0p).T.astype(f32))
    for l in range(2):
        S[f"l3f_wqkv_{l}"] = _c(S["l3f_wqkv"][l])
        S[f"l3f_qs_{l}"] = _c(S["l3f_qs"][l])
        S[f"l3f_ks_{l}"] = _c(S["l3f_ks"][l])
        S[f"l3f_wo_{l}"] = _c(S["l3f_wo"][l])
        S[f"l3f_lng_{l}"] = _c(S["l3f_lng"][l])
        S[f"l3f_lnb_{l}"] = _c(S["l3f_lnb"][l])
        S[f"l3f_w1a_{l}"] = _c(S["l3f_w1a"][l])
        S[f"l3f_w1g_{l}"] = _c(S["l3f_w1g"][l])
        S[f"l3f_b1a_{l}"] = _c(S["l3f_b1a"][l])
        S[f"l3f_b1g_{l}"] = _c(S["l3f_b1g"][l])
        S[f"l3f_w2_{l}"] = _c(S["l3f_w2"][l])
        S[f"l3f_b2_{l}"] = _c(S["l3f_b2"][l])
    tk8 = np.arange(128)[None, :, None]
    tq8 = np.arange(512)[None, None, :]
    kbp = np.arange(4)[:, None, None]
    for c in range(NCORE):
        h = c % 2
        mk = np.zeros((4, 8, 128, 512), BFNP)
        for w in range(4):
            W = 4 * h + w
            prev_ok = 1.0 if W > 0 else 0.0
            mk[w, :4] = (prev_ok * (tq8 <= kbp * 128 + tk8)).astype(f32)
            mk[w, 4:] = ((kbp * 128 + tk8) <= tq8).astype(f32)
        P[c]["l3f_mask"] = mk
        P[c]["gate_h"] = np.full((1, 1), float(h), f32)
        Q0 = TOWN * h
        P[c]["l3f_csq_q"] = _c(S["l3f_csq"][:, Q0:Q0 + TOWN])
        P[c]["l3f_ssq_q"] = _c(S["l3f_ssq"][:, Q0:Q0 + TOWN])
        P[c]["l3f_csk_own"] = _c(S["l3f_csk"][:, Q0:Q0 + TOWN])
        P[c]["l3f_ssk_own"] = _c(S["l3f_ssk"][:, Q0:Q0 + TOWN])
        P[c]["l3f_cskp_own"] = _c(S["l3f_cskp"][:, Q0:Q0 + TOWN])
        P[c]["l3f_sskp_own"] = _c(S["l3f_sskp"][:, Q0:Q0 + TOWN])
        pw = (Q0 - 512) % T
        P[c]["l3f_csk_prev"] = _c(S["l3f_cskp"][:, pw:pw + 512])
        P[c]["l3f_ssk_prev"] = _c(S["l3f_sskp"][:, pw:pw + 512])

    return S, P



class Bld:
    def __init__(self, nc, p):
        self.nc = nc
        self.p = p
        self.banks = [p.psum(f"bank{i}", [128, 512]) for i in range(7)]
        self.bank_bf = p.psum("bankbf", [128, 512], BF16)
        self.ones128 = p.sbuf("ones128", [128, 128], BF16)
        self.ones1 = p.sbuf("ones1", [1, 128], BF16)
        self.onesneg = p.sbuf("onesneg", [1, 128], BF16)
        p.memset("dve", self.ones128, self.ones128.ap, 1.0)
        p.memset("dve", self.ones1, self.ones1.ap, 1.0)
        p.memset("dve", self.onesneg, self.onesneg.ap, -1.0)
        self.ins = {}
        self.dram_tiles = {}
        self.prefix = ""
        self.inst_h = None
        self.fused_x = None

    def din(self, name, shape, dtype=F32):
        if self.fused_x is not None and name in ("x_own", "x_full", "x_orig"):
            return self.fused_x[name]
        full = self.prefix + name
        ap = self.nc.dram_tensor(full, list(shape), dtype,
                                 kind="ExternalInput").ap()
        self.ins[full] = ap
        return ap

    def dscratch(self, name, shape, dtype):
        ap = self.nc.dram_tensor(self.prefix + name, list(shape), dtype,
                                 kind="Internal").ap()
        t = self.p.dram_tile(ap, name)
        self.dram_tiles[name] = t
        return t

    def dout(self, name, shape, dtype=F32):
        if self.fused_x is not None and name == "x_out":
            t = self.fused_x["x_out"]
            self.dram_tiles[name] = t
            return t
        ap = self.nc.dram_tensor(name, list(shape), dtype,
                                 kind="ExternalOutput").ap()
        t = self.p.dram_tile(ap, name)
        self.dram_tiles[name] = t
        return t

    def emit_ln(self, xpre_tile, xpre_ap, store_fn, scratch, gb=None):
        p = self.p
        (c16, sq, mb, mb2, var, sdt, rstd, mrstd, ores) = scratch
        bS, bQ = self.banks[5], self.banks[6]
        for c in range(8):
            ct = c16[c % 2]
            st_ = sq[c % 2]
            p.cp("gp", ct, ct.ap, xpre_ap(c), reads=(xpre_tile,), tag="ln_cast")
            p.act(st_, st_.ap, ct.ap, AF.Square, reads=(ct,), tag="ln_sq")
            p.mm(bS, bS.ap, self.ones128.ap, ct.ap,
                 start=(c == 0), stop=(c == 7), reads=(ct, self.ones128),
                 tag="ln_sum")
            p.mm(bQ, bQ.ap, self.ones128.ap, st_.ap,
                 start=(c == 0), stop=(c == 7), reads=(st_, self.ones128),
                 tag="ln_sumsq")
        p.ts("dve", mb, mb.ap, bS.ap, 1.0 / D, ALU.mult, reads=(bS,), tag="ln_m")
        p.act(mb2, mb2.ap, mb.ap, AF.Square, reads=(mb,), tag="ln_m2")
        p.stt("dve", var, var.ap, bQ.ap, 1.0 / D, mb2.ap, ALU.mult, ALU.subtract,
              reads=(bQ, mb2), tag="ln_var")
        p.ts("dve", var, var.ap, var.ap, EPS, ALU.add, tag="ln_vep")
        p.rsqrt_pos(rstd, rstd.ap, var.ap, reads=(var,), tag="ln_rstd")
        p.tt("dve", mrstd, mrstd.ap, mb.ap, rstd.ap, ALU.mult, reads=(mb, rstd),
             tag="ln_mr")
        for c in range(8):
            ot = ores[c % 2]
            p.tt("dve", ot, ot.ap, xpre_ap(c), rstd.ap, ALU.mult,
                 reads=(xpre_tile, rstd), tag="ln_ap1")
            p.tt("dve", ot, ot.ap, ot.ap, mrstd.ap, ALU.subtract,
                 reads=(mrstd,), tag="ln_ap2")
            if gb is not None:
                ga, ba = gb(c)
                p.ts("dve", ot, ot.ap, ot.ap, ga, ALU.mult, ba, ALU.add,
                     tag="ln_gb")
            store_fn(c, ot.ap, ot)

    def ln_scratch(self):
        p = self.p
        return ([p.sbuf(f"ln_c16{i}", [128, 512], BF16) for i in range(2)],
                [p.sbuf(f"ln_sq{i}", [128, 512], BF16) for i in range(2)],
                p.sbuf("ln_mb", [128, 512], F32),
                p.sbuf("ln_mb2", [128, 512], F32),
                p.sbuf("ln_var", [128, 512], F32),
                p.sbuf("ln_sdt", [128, 512], F32),
                p.sbuf("ln_rstd", [128, 512], F32),
                p.sbuf("ln_mrstd", [128, 512], F32),
                [p.sbuf(f"ln_o{i}", [128, 512], F32) for i in range(2)])



def emit_l0f(bld):
    nc, p = bld.nc, bld.p
    SC = 1.0 / 127.0
    xo = bld.din("x_own", [D, TOWN], F32)
    wgT = bld.din("l0f_wgT", [D, 3072], I32)
    wuT = bld.din("l0f_wuT", [D, 3072], I32)
    wdT = bld.din("l0f_wdT", [3072, D], I32)
    xout = bld.dout("x_out", [D, TOWN], F32)

    FS = 768
    NSEC = 3072 // FS
    NFC = 3072 // 128
    FCS = FS // 128

    with p.scope():
        x16 = p.sbuf("x16", [128, 8 * 1024], BF16)
        h16 = p.sbuf("h16", [128, NFC * 1024], BF16)
        wgb = [p.sbuf(f"wgb{i}", [128, 8 * FS], BF16) for i in range(2)]
        wub = [p.sbuf(f"wub{i}", [128, 8 * FS], BF16) for i in range(2)]
        wdb = [p.sbuf(f"wdb{i}", [128, NFC * 128], BF16) for i in range(2)]
        x32b = [p.sbuf(f"x32b{i}", [128, 512], F32) for i in range(2)]
        silb = [p.sbuf(f"silb{i}", [128, 512], F32) for i in range(2)]
        xpre = p.sbuf("xpre", [128, 8 * 1024], F32)
        lns = bld.ln_scratch()

        for th in range(2):
            t0 = th * 1024
            p.dma("gp", x16.ap.rearrange("p (c t) -> p c t", c=8),
                  xo.rearrange("(c p) t -> p c t", p=128)[:, :, t0:t0 + 1024],
                  writes=(x16,), tag="x16")
            for s in range(NSEC):
                wg_t, wu_t = wgb[s % 2], wub[s % 2]
                p.dma("gp", wg_t.ap.rearrange("p (c f) -> p c f", c=8),
                      wgT.rearrange("(c p) f -> p c f", p=128)[:, :, s * FS:(s + 1) * FS],
                      writes=(wg_t,), tag="wg")
                p.dma("gp", wu_t.ap.rearrange("p (c f) -> p c f", c=8),
                      wuT.rearrange("(c p) f -> p c f", p=128)[:, :, s * FS:(s + 1) * FS],
                      writes=(wu_t,), tag="wu")
                for fl in range(FCS):
                    Fg = s * FCS + fl
                    for q2 in range(2):
                        bG, bU = bld.banks[(q2 * 2) % 4], bld.banks[(q2 * 2 + 1) % 4]
                        for dd in range(8):
                            p.mm(bG, bG.ap,
                                 wg_t.ap[:, dd * FS + fl * 128: dd * FS + (fl + 1) * 128],
                                 x16.ap[:, dd * 1024 + q2 * 512: dd * 1024 + (q2 + 1) * 512],
                                 start=(dd == 0), stop=(dd == 7),
                                 reads=(wg_t, x16), tag="gmm")
                        for dd in range(8):
                            p.mm(bU, bU.ap,
                                 wu_t.ap[:, dd * FS + fl * 128: dd * FS + (fl + 1) * 128],
                                 x16.ap[:, dd * 1024 + q2 * 512: dd * 1024 + (q2 + 1) * 512],
                                 start=(dd == 0), stop=(dd == 7),
                                 reads=(wu_t, x16), tag="umm")
                        sil = silb[q2]
                        p.act(sil, sil.ap, bG.ap, AF.Silu, scale=SC,
                              reads=(bG,), tag="silu")
                        p.stt("dve", h16,
                              h16.ap[:, Fg * 1024 + q2 * 512: Fg * 1024 + (q2 + 1) * 512],
                              bU.ap, SC, sil.ap, ALU.mult, ALU.mult,
                              reads=(bU, sil), tag="hmul")
            for dd in range(8):
                wd_t = wdb[dd % 2]
                p.dma("gp", wd_t.ap.rearrange("p (f e) -> p f e", f=NFC),
                      wdT.rearrange("(f p) e -> p f e", p=128)[:, :, dd * 128:(dd + 1) * 128],
                      writes=(wd_t,), tag="wd")
                for q2 in range(2):
                    bD = bld.banks[4]
                    for ff in range(NFC):
                        p.mm(bD, bD.ap,
                             wd_t.ap[:, ff * 128:(ff + 1) * 128],
                             h16.ap[:, ff * 1024 + q2 * 512: ff * 1024 + (q2 + 1) * 512],
                             start=(ff == 0), stop=(ff == NFC - 1),
                             reads=(wd_t, h16), tag="dmm")
                        if ff == 0:
                            xb = x32b[q2]
                            p.dma("sync", xb.ap,
                                  xo[dd * 128:(dd + 1) * 128,
                                     t0 + q2 * 512: t0 + (q2 + 1) * 512],
                                  writes=(xb,), tag="x32")
                    p.stt("dve", xpre,
                          xpre.ap[:, dd * 1024 + q2 * 512: dd * 1024 + (q2 + 1) * 512],
                          bD.ap, SC, x32b[q2].ap, ALU.mult, ALU.add,
                          reads=(bD, x32b[q2]), tag="dres")
            for q2 in range(2):
                def xpre_ap(c, _q2=q2):
                    return xpre.ap[:, c * 1024 + _q2 * 512: c * 1024 + (_q2 + 1) * 512]

                def store(cc, oap, o32t, _t0=t0, _q2=q2):
                    p.dma("sync", xout[cc * 128:(cc + 1) * 128,
                                       _t0 + _q2 * 512: _t0 + (_q2 + 1) * 512],
                          oap, reads=(o32t,), writes=(bld.dram_tiles["x_out"],),
                          tag="xout")
                bld.emit_ln(xpre, xpre_ap, store, lns)

    p.op("sync", lambda eng: eng.nop(),
         reads=(bld.dram_tiles["x_out"],), tag="final_wait")




def emit_l0a(bld):
    nc, p = bld.nc, bld.p
    BETA = 1.0 / 32.0
    xo = bld.din("x_own", [D, TOWN], F32)
    xf = bld.din("x_full", [2, D, TOWN], F32)
    wqT = bld.din("l0a_wqT", [D, D], F32)
    wkT = bld.din("l0a_wkT", [D, D], F32)
    wvT = bld.din("l0a_wvT", [D, D], F32)
    cosk = bld.din("l0a_cosk", [512, T], F32)
    sink = bld.din("l0a_sink", [512, T], F32)
    cosq = bld.din("l0a_cosq", [512, TOWN], F32)
    sinq = bld.din("l0a_sinq", [512, TOWN], F32)
    maskd = bld.din("l0a_mask", [NQC, 32, 128, 512], BF16)
    xout = bld.dout("x_out", [D, TOWN], F32)
    kdram = bld.dscratch("kdram", [32, 128, 8 * 128], BF16)
    vdram = bld.dscratch("vdram", [32, 128, 1024], BF16)

    def xfull_ap(t0, rows=None):
        hh = t0 // TOWN
        off = t0 % TOWN
        v = xf[hh].rearrange("(c p) t -> p c t", p=128)
        return v[:, :, off:off + 512]

    kmaxs = p.sbuf("kmaxs", [1, 1], F32)
    wq16 = p.sbuf("wq16", [128, 8 * D], BF16)
    p.dma("gp", wq16.ap.rearrange("p (c e) -> p c e", c=8),
          wqT.rearrange("(c p) e -> p c e", p=128), writes=(wq16,), tag="wq")
    with p.scope():
        wk16 = p.sbuf("wk16", [128, 8 * D], BF16)
        wv16 = p.sbuf("wv16", [128, 8 * D], BF16)
        for nm, t_, ap in (("wk", wk16, wkT), ("wv", wv16, wvT)):
            p.dma("gp", t_.ap.rearrange("p (c e) -> p c e", c=8),
                  ap.rearrange("(c p) e -> p c e", p=128), writes=(t_,), tag=nm)
        xkb = [p.sbuf(f"xkb{i}", [128, 8 * 512], BF16) for i in range(2)]
        kraw = p.sbuf("kraw", [128, 8 * 512], F32)
        krot = p.sbuf("krot", [128, 8 * 512], BF16)
        ksq = p.sbuf("ksq", [128, 8 * 512], BF16)
        cosb = [p.sbuf(f"cosb{i}", [128, 512], F32) for i in range(4)]
        sinb = [p.sbuf(f"sinb{i}", [128, 512], F32) for i in range(4)]
        rt = [p.sbuf(f"rt{i}", [128, 512], F32) for i in range(4)]
        vstage = [p.sbuf(f"vstage{i}", [128, 1024], BF16) for i in range(2)]
        knorm2 = p.sbuf("knorm2", [1, T], F32)
        kmaxs0 = p.sbuf("kmaxs0", [1, 1], F32)

        def rope_rotate(src32, dst16, cos_t, sin_t, reads_extra):
            for cc in range(4):
                e_ap = src32.ap[:, cc * 512:(cc + 1) * 512]
                o_ap = src32.ap[:, (cc + 4) * 512:(cc + 5) * 512]
                p.tt("gp", rt[0], rt[0].ap, e_ap, cos_t[cc].ap, ALU.mult,
                     reads=(src32, cos_t[cc]), tag="rp1")
                p.tt("gp", rt[1], rt[1].ap, o_ap, sin_t[cc].ap, ALU.mult,
                     reads=(src32, sin_t[cc]), tag="rp2")
                p.tt("gp", rt[2], rt[2].ap, e_ap, sin_t[cc].ap, ALU.mult,
                     reads=(src32, sin_t[cc]), tag="rp3")
                p.tt("gp", rt[3], rt[3].ap, o_ap, cos_t[cc].ap, ALU.mult,
                     reads=(src32, cos_t[cc]), tag="rp4")
                p.tt("dve", dst16, dst16.ap[:, cc * 512:(cc + 1) * 512],
                     rt[0].ap, rt[1].ap, ALU.subtract, reads=(rt[0], rt[1]),
                     tag="rp5")
                p.tt("dve", dst16, dst16.ap[:, (cc + 4) * 512:(cc + 5) * 512],
                     rt[2].ap, rt[3].ap, ALU.add, reads=(rt[2], rt[3]), tag="rp6")

        cos_t = [None] * 4
        sin_t = [None] * 4
        NG_K = 8 if bld.inst_h is None else (5 + 3 * bld.inst_h)
        for g in range(NG_K):
            t0 = g * 512
            xk = xkb[g % 2]
            p.dma("gp", xk.ap.rearrange("p (c t) -> p c t", c=8), xfull_ap(t0),
                  writes=(xk,), tag="xk")
            for ec in range(8):
                bK = bld.banks[ec % 2]
                for dd in range(8):
                    p.mm(bK, bK.ap,
                         wk16.ap[:, dd * D + ec * 128: dd * D + (ec + 1) * 128],
                         xk.ap[:, dd * 512:(dd + 1) * 512],
                         start=(dd == 0), stop=(dd == 7), reads=(wk16, xk),
                         tag="kmm")
                p.cp("dve", kraw, kraw.ap[:, ec * 512:(ec + 1) * 512], bK.ap,
                     reads=(bK,), tag="kev")
            for cc in range(4):
                cb, sb = cosb[cc], sinb[cc]
                p.dma("sync", cb.ap, cosk[cc * 128:(cc + 1) * 128, t0:t0 + 512],
                      writes=(cb,), tag="cos")
                p.dma("sync", sb.ap, sink[cc * 128:(cc + 1) * 128, t0:t0 + 512],
                      writes=(sb,), tag="sin")
                cos_t[cc], sin_t[cc] = cb, sb
            rope_rotate(kraw, krot, cos_t, sin_t, ())
            p.act(ksq, ksq.ap, krot.ap, AF.Square, reads=(krot,), tag="ksq")
            bN = bld.banks[4]
            for ec in range(8):
                p.mm(bN, bN.ap, bld.ones128.ap,
                     ksq.ap[:, ec * 512:(ec + 1) * 512],
                     start=(ec == 0), stop=(ec == 7), reads=(ksq, bld.ones128),
                     tag="nmm")
            p.cp("dve", knorm2, knorm2.ap[0:1, t0:t0 + 512], bN.ap[0:1, :],
                 reads=(bN,), tag="knrow")
            kr4 = krot.ap.rearrange("p (c s k) -> p s c k", c=8, s=4)
            for s_ in range(4):
                p.dma("sync",
                      kdram.ap[4 * g + s_].rearrange("p (c k) -> p c k", c=8),
                      kr4[:, s_], reads=(krot,),
                      writes=(bld.dram_tiles["kdram"],), tag="kst")
            for s_ in range(4):
                vst = vstage[s_ % 2]
                for hf in range(2):
                    bV = bld.banks[2 + (hf % 2)]
                    for dd in range(8):
                        p.mm(bV, bV.ap,
                             xk.ap[:, dd * 512 + s_ * 128: dd * 512 + (s_ + 1) * 128],
                             wv16.ap[:, dd * D + hf * 512: dd * D + (hf + 1) * 512],
                             start=(dd == 0), stop=(dd == 7), reads=(xk, wv16),
                             tag="vmm")
                    p.cp("dve", vst, vst.ap[:, hf * 512:(hf + 1) * 512], bV.ap,
                         reads=(bV,), tag="vev")
                p.dma("sync", vdram.ap[4 * g + s_], vst.ap, reads=(vst,),
                      writes=(bld.dram_tiles["vdram"],), tag="vst")
        p.red(kmaxs0, kmaxs0.ap, knorm2.ap[0:1, 0:NG_K * 512], AX.X, ALU.max,
              reads=(knorm2,), tag="kmax_r")
        p.sqrt_pos(kmaxs, kmaxs.ap, kmaxs0.ap, reads=(kmaxs0,),
                   tag="kmax_s")

    with p.scope():
        qraw = p.sbuf("qraw", [128, 8 * 512], F32)
        qrot = p.sbuf("qrot", [128, 8 * 512], BF16)
        qsq = p.sbuf("qsq", [128, 8 * 512], BF16)
        xqb = p.sbuf("xqb", [128, 8 * 512], BF16)
        cosb = [p.sbuf(f"qcosb{i}", [128, 512], F32) for i in range(4)]
        sinb = [p.sbuf(f"qsinb{i}", [128, 512], F32) for i in range(4)]
        rt = [p.sbuf(f"qrt{i}", [128, 512], F32) for i in range(4)]
        qn = p.sbuf("qn", [1, 512], F32)
        crow = p.sbuf("crow", [1, 512], BF16)
        ktile = [p.sbuf(f"ktile{i}", [128, 8 * 128], BF16) for i in range(2)]
        vtile = [p.sbuf(f"vtile{i}", [128, 1024], BF16) for i in range(8)]
        mtile = [p.sbuf(f"mtile{i}", [128, 512], BF16) for i in range(2)]
        ptile = [p.sbuf(f"ptile{i}", [128, 512], BF16) for i in range(8)]
        o32 = p.sbuf("o32", [128, 8 * 512], F32)
        linv = p.sbuf("linv", [1, 512], F32)
        linv16 = p.sbuf("linv16", [1, 512], BF16)
        lb16 = p.sbuf("lb16", [128, 512], BF16)
        xres = [p.sbuf(f"xres{i}", [128, 512], F32) for i in range(2)]
        xpre = p.sbuf("xpre", [128, 8 * 512], F32)
        lns = bld.ln_scratch()

        def rope_rotate_q(src32, dst16, cos_t, sin_t):
            for cc in range(4):
                e_ap = src32.ap[:, cc * 512:(cc + 1) * 512]
                o_ap = src32.ap[:, (cc + 4) * 512:(cc + 5) * 512]
                p.tt("gp", rt[0], rt[0].ap, e_ap, cos_t[cc].ap, ALU.mult,
                     reads=(src32, cos_t[cc]), tag="qr1")
                p.tt("gp", rt[1], rt[1].ap, o_ap, sin_t[cc].ap, ALU.mult,
                     reads=(src32, sin_t[cc]), tag="qr2")
                p.tt("gp", rt[2], rt[2].ap, e_ap, sin_t[cc].ap, ALU.mult,
                     reads=(src32, sin_t[cc]), tag="qr3")
                p.tt("gp", rt[3], rt[3].ap, o_ap, cos_t[cc].ap, ALU.mult,
                     reads=(src32, cos_t[cc]), tag="qr4")
                p.tt("dve", dst16, dst16.ap[:, cc * 512:(cc + 1) * 512],
                     rt[0].ap, rt[1].ap, ALU.subtract, reads=(rt[0], rt[1]),
                     tag="qr5")
                p.tt("dve", dst16, dst16.ap[:, (cc + 4) * 512:(cc + 5) * 512],
                     rt[2].ap, rt[3].ap, ALU.add, reads=(rt[2], rt[3]), tag="qr6")

        for qc in range(NQC):
            q0 = qc * 512
            p.dma("gp", xqb.ap.rearrange("p (c t) -> p c t", c=8),
                  xo.rearrange("(c p) t -> p c t", p=128)[:, :, q0:q0 + 512],
                  writes=(xqb,), tag="xq")
            for ec in range(8):
                bQ = bld.banks[ec % 2]
                for dd in range(8):
                    p.mm(bQ, bQ.ap,
                         wq16.ap[:, dd * D + ec * 128: dd * D + (ec + 1) * 128],
                         xqb.ap[:, dd * 512:(dd + 1) * 512],
                         start=(dd == 0), stop=(dd == 7), reads=(wq16, xqb),
                         tag="qmm")
                p.cp("dve", qraw, qraw.ap[:, ec * 512:(ec + 1) * 512], bQ.ap,
                     reads=(bQ,), tag="qev")
            cos_t = [None] * 4
            sin_t = [None] * 4
            for cc in range(4):
                cb, sb = cosb[cc], sinb[cc]
                p.dma("sync", cb.ap, cosq[cc * 128:(cc + 1) * 128, q0:q0 + 512],
                      writes=(cb,), tag="qcos")
                p.dma("sync", sb.ap, sinq[cc * 128:(cc + 1) * 128, q0:q0 + 512],
                      writes=(sb,), tag="qsin")
                cos_t[cc], sin_t[cc] = cb, sb
            rope_rotate_q(qraw, qrot, cos_t, sin_t)
            p.act(qsq, qsq.ap, qrot.ap, AF.Square, reads=(qrot,), tag="qsq")
            bN = bld.banks[4]
            for ec in range(8):
                p.mm(bN, bN.ap, bld.ones128.ap,
                     qsq.ap[:, ec * 512:(ec + 1) * 512],
                     start=(ec == 0), stop=(ec == 7), reads=(qsq, bld.ones128),
                     tag="qnmm")
            p.sqrt_pos(qn, qn.ap, bN.ap[0:1, :], reads=(bN,), tag="qn_s")
            p.ts("dve", crow, crow.ap, qn.ap, kmaxs.ap[0:1, 0:1], ALU.mult,
                 reads=(qn, kmaxs), tag="crow")
            p.memset("dve", o32, o32.ap, 0.0)
            bL = bld.banks[4]
            first_l = True
            if bld.inst_h is None:
                nkb = 32
            else:
                nkb = min(32, 4 * (qc + 1) + 16 * bld.inst_h + 1)
            nkg = (nkb + 7) // 8
            for kg in range(nkg):
                gsz = min(8, nkb - 8 * kg)
                for ks in range(gsz):
                    kt = kg * 8 + ks
                    ktl = ktile[kt % 2]
                    mtl = mtile[kt % 2]
                    p.dma("sync", ktl.ap, kdram.ap[kt],
                          reads=(bld.dram_tiles["kdram"],), writes=(ktl,),
                          tag="ktl")
                    p.dma("sync", vtile[ks].ap, vdram.ap[kt],
                          reads=(bld.dram_tiles["vdram"],), writes=(vtile[ks],),
                          tag="vtl")
                    if bld.inst_h is None or kt >= nkb - 5:
                        p.dma("sync", mtl.ap, maskd[qc, kt],
                              writes=(mtl,), tag="mtl")
                    bS = bld.banks[2 + kt % 2]
                    for cc in range(8):
                        p.mm(bS, bS.ap,
                             ktl.ap[:, cc * 128:(cc + 1) * 128],
                             qrot.ap[:, cc * 512:(cc + 1) * 512],
                             start=(cc == 0), stop=False, reads=(ktl, qrot),
                             tag="smm")
                    p.mm(bS, bS.ap, bld.onesneg.ap, crow.ap,
                         start=False, stop=True, reads=(bld.onesneg, crow),
                         tag="cmm")
                    pt = ptile[ks]
                    p.act(pt, pt.ap, bS.ap, AF.Exp, scale=BETA, reads=(bS,),
                          tag="pexp")
                    if bld.inst_h is None or kt >= nkb - 5:
                        p.tt("gp", pt, pt.ap, pt.ap, mtl.ap, ALU.mult,
                             reads=(mtl,), tag="pmask")
                for ks in range(gsz):
                    kt = kg * 8 + ks
                    p.mm(bL, bL.ap[0:1, :], bld.ones128.ap[:, 0:1],
                         ptile[ks].ap, start=first_l, stop=(kt == nkb - 1),
                         reads=(ptile[ks], bld.ones128), tag="lmm")
                    first_l = False
                for dv in range(8):
                    bP = bld.banks[5 + (dv % 2)]
                    for ks in range(gsz):
                        p.mm(bP, bP.ap, vtile[ks].ap[:, dv * 128:(dv + 1) * 128],
                             ptile[ks].ap, start=(ks == 0), stop=(ks == gsz - 1),
                             reads=(vtile[ks], ptile[ks]), tag="pvmm")
                    p.tt("dve", o32, o32.ap[:, dv * 512:(dv + 1) * 512],
                         o32.ap[:, dv * 512:(dv + 1) * 512], bP.ap, ALU.add,
                         reads=(bP,), tag="oacc")
            p.recip(linv, linv.ap, bL.ap[0:1, :], reads=(bL,), tag="linv")
            p.cp("dve", linv16, linv16.ap, linv.ap, reads=(linv,), tag="linv16")
            bB = bld.banks[2]
            p.mm(bB, bB.ap, bld.ones1.ap, linv16.ap, start=True, stop=True,
                 reads=(bld.ones1, linv16), tag="lbc")
            p.cp("dve", lb16, lb16.ap, bB.ap, reads=(bB,), tag="lbc16")
            for dv in range(8):
                xr = xres[dv % 2]
                p.dma("sync", xr.ap,
                      xo[dv * 128:(dv + 1) * 128, q0:q0 + 512], writes=(xr,),
                      tag="xres")
                p.tt("dve", xpre, xpre.ap[:, dv * 512:(dv + 1) * 512],
                     o32.ap[:, dv * 512:(dv + 1) * 512], lb16.ap, ALU.mult,
                     reads=(o32, lb16), tag="onorm")
                p.tt("dve", xpre, xpre.ap[:, dv * 512:(dv + 1) * 512],
                     xpre.ap[:, dv * 512:(dv + 1) * 512], xr.ap, ALU.add,
                     reads=(xr,), tag="ores")

            def xpre_ap(c, _qc=qc):
                return xpre.ap[:, c * 512:(c + 1) * 512]

            def store(cc, oap, o32t, _q0=q0):
                p.dma("sync", xout[cc * 128:(cc + 1) * 128, _q0:_q0 + 512],
                      oap, reads=(o32t,), writes=(bld.dram_tiles["x_out"],),
                      tag="xout")
            bld.emit_ln(xpre, xpre_ap, store, lns)

    p.op("sync", lambda eng: eng.nop(),
         reads=(bld.dram_tiles["x_out"],), tag="final_wait")


def _spec_l0a():
    return (emit_l0a, [
        ("l0a_wqT", "l0a_wqT", [D, D], F32, False),
        ("l0a_wkT", "l0a_wkT", [D, D], F32, False),
        ("l0a_wvT", "l0a_wvT", [D, D], F32, False),
        ("l0a_cosk", "l0a_cosk", [512, T], F32, False),
        ("l0a_sink", "l0a_sink", [512, T], F32, False),
        ("l0a_cosq", "l0a_cosq", [512, TOWN], F32, True),
        ("l0a_sinq", "l0a_sinq", [512, TOWN], F32, True),
        ("l0a_mask", "l0a_mask", [NQC, 32, 128, 512], BF16, True),
    ], True)




def emit_mha(bld, cfg):
    nc, p = bld.nc, bld.p
    H, DH = 16, 64
    beta = cfg["beta"]
    gated = cfg["gated"]
    masked = cfg["masked"]
    causal = cfg["causal"] and bld.inst_h is not None
    NGRP = 4 * (bld.inst_h + 1) if causal else 8
    xo = bld.din("x_own", [D, TOWN], F32)
    xf = bld.din("x_full", [2, D, TOWN], F32)
    if gated:
        qkvT = bld.din("qkvT", [D, 3 * D], F32)
        qkvmT = bld.din("qkvmT", [D, 3 * D], F32)
        owT = bld.din("owT", [D, D], F32)
        owmT = bld.din("owmT", [D, D], F32)
        ob2 = bld.din("ob2", [128, 8], F32)
        obm2 = bld.din("obm2", [128, 8], F32)
    else:
        wq_in = bld.din("wq_in", [D, D], F32)
        wk_in = bld.din("wk_in", [D, D], F32)
        wv_in = bld.din("wv_in", [D, D], F32)
        owT = bld.din("ow_in", [D, D], F32)
        ob2 = bld.din("ob2", [128, 8], F32)
    if masked:
        maskd = bld.din("mask", [NQC, 32, 128, 512], BF16)
    xout = bld.dout("x_out", [D, TOWN], F32)
    kdram = bld.dscratch("kdram", [H, 65, T], BF16)
    vdram = bld.dscratch("vdram", [32, 128, H * 65], BF16)

    def xfull_ap(t0):
        hh = t0 // TOWN
        off = t0 % TOWN
        return xf[hh].rearrange("(c p) t -> p c t", p=128)[:, :, off:off + 512]

    kmhs = p.sbuf("kmhs", [1, 16], F32)
    q16 = p.sbuf("q16", [128, 8 * TOWN], BF16)
    ow16 = p.sbuf("ow16", [128, 8 * D], BF16)
    gb2 = p.sbuf("gb2", [128, 8], F32)

    with p.scope():
        qkv16 = p.sbuf("qkv16", [128, 8 * 3 * D], BF16)
        if gated:
          with p.scope():
            wtmp = [p.sbuf(f"wtmp{i}", [128, 8 * 512], F32) for i in range(2)]
            mtmp = [p.sbuf(f"mtmp{i}", [128, 8 * 512], F32) for i in range(2)]
            for s in range(6):
                wt, mt = wtmp[s % 2], mtmp[s % 2]
                p.dma("sync", wt.ap.rearrange("p (c e) -> p c e", c=8),
                      qkvT.rearrange("(c p) e -> p c e", p=128)[:, :, s * 512:(s + 1) * 512],
                      writes=(wt,), tag="qw")
                p.dma("sync", mt.ap.rearrange("p (c e) -> p c e", c=8),
                      qkvmT.rearrange("(c p) e -> p c e", p=128)[:, :, s * 512:(s + 1) * 512],
                      writes=(mt,), tag="qm")
                for cc in range(8):
                    p.stt("dve", qkv16,
                          qkv16.ap[:, cc * 3 * D + s * 512: cc * 3 * D + (s + 1) * 512],
                          mt.ap[:, cc * 512:(cc + 1) * 512], 0.0,
                          wt.ap[:, cc * 512:(cc + 1) * 512], ALU.is_gt, ALU.mult,
                          reads=(wt, mt), tag="qgate")
            for s in range(2):
                wt, mt = wtmp[s % 2], mtmp[s % 2]
                p.dma("sync", wt.ap.rearrange("p (c e) -> p c e", c=8),
                      owT.rearrange("(c p) e -> p c e", p=128)[:, :, s * 512:(s + 1) * 512],
                      writes=(wt,), tag="oww")
                p.dma("sync", mt.ap.rearrange("p (c e) -> p c e", c=8),
                      owmT.rearrange("(c p) e -> p c e", p=128)[:, :, s * 512:(s + 1) * 512],
                      writes=(mt,), tag="owm")
                for cc in range(8):
                    p.stt("dve", ow16,
                          ow16.ap[:, cc * D + s * 512: cc * D + (s + 1) * 512],
                          mt.ap[:, cc * 512:(cc + 1) * 512], 0.0,
                          wt.ap[:, cc * 512:(cc + 1) * 512], ALU.is_gt, ALU.mult,
                          reads=(wt, mt), tag="owgate")
            obt = p.sbuf("obt", [128, 8], F32)
            obmt = p.sbuf("obmt", [128, 8], F32)
            p.dma("sync", obt.ap, ob2, writes=(obt,), tag="ob")
            p.dma("sync", obmt.ap, obm2, writes=(obmt,), tag="obm")
            p.stt("dve", gb2, gb2.ap, obmt.ap, 0.0, obt.ap, ALU.is_gt, ALU.mult,
                  reads=(obt, obmt), tag="gbias")
        else:
            for i, w_in in enumerate((wq_in, wk_in, wv_in)):
                p.dma("gp", qkv16.ap.rearrange("p (c e) -> p c e", c=8, e=3 * D)[
                    :, :, i * D:(i + 1) * D],
                    w_in.rearrange("(c p) e -> p c e", p=128),
                    writes=(qkv16,), tag="wqkv")
            p.dma("gp", ow16.ap.rearrange("p (c e) -> p c e", c=8),
                  owT.rearrange("(c p) e -> p c e", p=128), writes=(ow16,),
                  tag="ow")
            obt = p.sbuf("obt", [128, 8], F32)
            p.dma("sync", gb2.ap, ob2, writes=(gb2,), tag="ob")

        onesrow = p.sbuf("onesrow", [1, T], BF16)
        p.memset("dve", onesrow, onesrow.ap, 1.0)
        for h in range(H):
            p.dma("sync", kdram.ap[h, 64:65, :], onesrow.ap, reads=(onesrow,),
                  writes=(bld.dram_tiles["kdram"],), tag="kones")

        xkb = [p.sbuf(f"xkb{i}", [128, 8 * 512], BF16) for i in range(2)]
        kstage = [p.sbuf(f"kstage{i}", [128, 512], BF16) for i in range(2)]
        ksq = p.sbuf("ksq", [128, 512], BF16)
        ktmp1 = p.sbuf("ktmp1", [1, 1], F32)
        vstage = [p.sbuf(f"vstage{i}", [128, H * 65], BF16) for i in range(2)]
        for i in range(2):
            p.memset("dve", vstage[i], vstage[i].ap, 1.0)
        p.memset("dve", kmhs, kmhs.ap, 0.0)

        for g in range(NGRP):
            t0 = g * 512
            xk = xkb[g % 2]
            p.dma("gp", xk.ap.rearrange("p (c t) -> p c t", c=8), xfull_ap(t0),
                  writes=(xk,), tag="xk")
            for ec in range(8):
                bK = bld.banks[ec % 2]
                for dd in range(8):
                    p.mm(bK, bK.ap,
                         qkv16.ap[:, dd * 3 * D + D + ec * 128:
                                  dd * 3 * D + D + (ec + 1) * 128],
                         xk.ap[:, dd * 512:(dd + 1) * 512],
                         start=(dd == 0), stop=(dd == 7), reads=(qkv16, xk),
                         tag="kmm")
                kst = kstage[ec % 2]
                p.cp("dve", kst, kst.ap, bK.ap, reads=(bK,), tag="kev")
                for sub in range(2):
                    h = 2 * ec + sub
                    p.dma("sync", kdram.ap[h, 0:64, t0:t0 + 512],
                          kst.ap[64 * sub:64 * (sub + 1), :], reads=(kst,),
                          writes=(bld.dram_tiles["kdram"],), tag="kst")
                p.act(ksq, ksq.ap, kst.ap, AF.Square, reads=(kst,), tag="ksq")
                for sub in range(2):
                    h = 2 * ec + sub
                    bN = bld.banks[6]
                    p.mm(bN, bN.ap[0:1, :],
                         bld.ones128.ap[64 * sub:64 * (sub + 1), 0:1],
                         ksq.ap[64 * sub:64 * (sub + 1), :],
                         start=True, stop=True, reads=(ksq, bld.ones128),
                         tag="knmm")
                    p.red(ktmp1, ktmp1.ap, bN.ap[0:1, :], AX.X, ALU.max,
                          reads=(bN,), tag="knred")
                    p.tt("dve", kmhs, kmhs.ap[0:1, h:h + 1],
                         kmhs.ap[0:1, h:h + 1], ktmp1.ap, ALU.max,
                         reads=(ktmp1,), tag="knupd")
            for s_ in range(4):
                vst = vstage[s_ % 2]
                for hf in range(2):
                    bV = bld.banks[2 + hf]
                    for dd in range(8):
                        p.mm(bV, bV.ap,
                             xk.ap[:, dd * 512 + s_ * 128: dd * 512 + (s_ + 1) * 128],
                             qkv16.ap[:, dd * 3 * D + 2 * D + hf * 512:
                                      dd * 3 * D + 2 * D + (hf + 1) * 512],
                             start=(dd == 0), stop=(dd == 7), reads=(xk, qkv16),
                             tag="vmm")
                        pass
                    p.cp("dve", vst,
                         vst.ap.rearrange("p (h e) -> p h e", h=H)[
                             :, hf * 8:(hf + 1) * 8, 0:64],
                         bV.ap.rearrange("p (h e) -> p h e", h=8),
                         reads=(bV,), tag="vev")
                p.dma("sync", vdram.ap[4 * g + s_], vst.ap, reads=(vst,),
                      writes=(bld.dram_tiles["vdram"],), tag="vst")
        p.sqrt_pos(kmhs, kmhs.ap, kmhs.ap, tag="kmsqrt")

        xq = p.sbuf("xq", [128, 8 * TOWN], BF16)
        p.dma("gp", xq.ap.rearrange("p (c t) -> p c t", c=8),
              xo.rearrange("(c p) t -> p c t", p=128), writes=(xq,), tag="xq")
        for ec in range(8):
            for qc in range(NQC):
                bQ = bld.banks[(ec * NQC + qc) % 2]
                for dd in range(8):
                    p.mm(bQ, bQ.ap,
                         qkv16.ap[:, dd * 3 * D + ec * 128:
                                  dd * 3 * D + (ec + 1) * 128],
                         xq.ap[:, dd * TOWN + qc * 512: dd * TOWN + (qc + 1) * 512],
                         start=(dd == 0), stop=(dd == 7), reads=(qkv16, xq),
                         tag="qmm")
                p.cp("dve", q16,
                     q16.ap[:, ec * TOWN + qc * 512: ec * TOWN + (qc + 1) * 512],
                     bQ.ap, reads=(bQ,), tag="qev")

    with p.scope():
        attnout = p.sbuf("attnout", [128, 8 * TOWN], BF16)
        khg = p.sbuf("khg", [65, 4 * T], BF16)
        qstage = [p.sbuf(f"qstage{i}", [65, 512], BF16) for i in range(4)]
        qsqs = p.sbuf("qsqs", [128, 512], BF16)
        qn = p.sbuf("qn", [1, 512], F32)
        vt4 = [p.sbuf(f"vt4{i}", [128, 4 * 65], BF16) for i in range(2)]
        mtile = [p.sbuf(f"mtile{i}", [128, 512], BF16) for i in range(2)]
        ptile = [p.sbuf(f"ptile{i}", [128, 512], BF16) for i in range(3)]
        linv = p.sbuf("linv", [1, 512], F32)
        linv16 = p.sbuf("linv16", [1, 512], BF16)
        lb16 = p.sbuf("lb16", [64, 512], BF16)
        for hg in range(4):
            for i4 in range(4):
                h = 4 * hg + i4
                p.dma("sync", khg.ap[:, i4 * T: i4 * T + NGRP * 512],
                      kdram.ap[h][:, 0:NGRP * 512],
                      reads=(bld.dram_tiles["kdram"],), writes=(khg,), tag="khg")
            for qc in range(NQC):
                q0 = qc * 512
                for i4 in range(4):
                    h = 4 * hg + i4
                    cchunk, coff = h // 2, 64 * (h % 2)
                    qsl = q16.ap[:, cchunk * TOWN + q0: cchunk * TOWN + q0 + 512]
                    qs = qstage[i4]
                    p.cp("dve", qs, qs.ap[0:64, :],
                         qsl[coff:coff + 64, :] if coff == 0 else
                         q16.ap[coff:coff + 64,
                                cchunk * TOWN + q0: cchunk * TOWN + q0 + 512],
                         reads=(q16,), tag="qstg")
                    p.act(qsqs, qsqs.ap[coff:coff + 64, :] if False else
                          qsqs.ap[0:64, :],
                          qs.ap[0:64, :], AF.Square, reads=(qs,), tag="qsq")
                    bN = bld.banks[6]
                    p.mm(bN, bN.ap[0:1, :], bld.ones128.ap[0:64, 0:1],
                         qsqs.ap[0:64, :], start=True, stop=True,
                         reads=(qsqs, bld.ones128), tag="qnmm")
                    p.sqrt_pos(qn, qn.ap, bN.ap[0:1, :], reads=(bN,),
                               tag="qns")
                    p.ts("dve", qs, qs.ap[64:65, :], qn.ap,
                         kmhs.ap[0:1, h:h + 1], ALU.mult, -1.0, ALU.mult,
                         reads=(qn, kmhs), tag="crow")
                nkb = (4 * (qc + 1) + 16 * bld.inst_h) if causal else 32
                for kt in range(nkb):
                    vt = vt4[kt % 2]
                    p.dma("sync", vt.ap,
                          vdram.ap[kt][:, 4 * hg * 65:(4 * hg + 4) * 65],
                          reads=(bld.dram_tiles["vdram"],), writes=(vt,),
                          tag="vt4")
                    domask = masked and (not causal or kt >= nkb - 4)
                    if domask:
                        mtl = mtile[kt % 2]
                        p.dma("sync", mtl.ap, maskd[qc, kt], writes=(mtl,),
                              tag="mtl")
                    for i4 in range(4):
                        h = 4 * hg + i4
                        bS = bld.banks[4 + (kt + i4) % 2]
                        p.mm(bS, bS.ap,
                             khg.ap[:, i4 * T + kt * 128: i4 * T + (kt + 1) * 128],
                             qstage[i4].ap, start=True, stop=True,
                             reads=(khg, qstage[i4]), tag="smm")
                        pt = ptile[(kt * 4 + i4) % 3]
                        p.act(pt, pt.ap, bS.ap, AF.Exp, scale=beta, reads=(bS,),
                              tag="pexp")
                        if domask:
                            p.tt("gp", pt, pt.ap, pt.ap, mtl.ap, ALU.mult,
                                 reads=(mtl,), tag="pmask")
                        bO = bld.banks[i4]
                        p.mm(bO, bO.ap[0:65, :],
                             vt.ap[:, i4 * 65:(i4 + 1) * 65], pt.ap,
                             start=(kt == 0), stop=(kt == nkb - 1),
                             reads=(vt, pt), tag="pvmm")
                for i4 in range(4):
                    h = 4 * hg + i4
                    bO = bld.banks[i4]
                    p.recip(linv, linv.ap, bO.ap[64:65, :], reads=(bO,),
                            tag="linv")
                    p.cp("dve", linv16, linv16.ap, linv.ap, reads=(linv,),
                         tag="linv16")
                    bB = bld.banks[4]
                    p.mm(bB, bB.ap[0:64, :], bld.ones1.ap[0:1, 0:64],
                         linv16.ap, start=True, stop=True,
                         reads=(bld.ones1, linv16), tag="lbc")
                    p.cp("dve", lb16, lb16.ap, bB.ap[0:64, :], reads=(bB,),
                         tag="lbc16")
                    cchunk, coff = h // 2, 64 * (h % 2)
                    p.tt("dve", attnout,
                         attnout.ap[coff:coff + 64,
                                    cchunk * TOWN + q0: cchunk * TOWN + q0 + 512],
                         bO.ap[0:64, :], lb16.ap, ALU.mult, reads=(bO, lb16),
                         tag="onorm")

        xres = [p.sbuf(f"xres{i}", [128, 512], F32) for i in range(2)]
        xpre = p.sbuf("xpre", [128, 8 * 512], F32)
        tmp1 = p.sbuf("tmp1", [128, 512], F32)
        lns = bld.ln_scratch()
        for qc in range(NQC):
            q0 = qc * 512
            for oc in range(8):
                bP = bld.banks[oc % 2]
                for ic in range(8):
                    p.mm(bP, bP.ap,
                         ow16.ap[:, ic * D + oc * 128: ic * D + (oc + 1) * 128],
                         attnout.ap[:, ic * TOWN + q0: ic * TOWN + q0 + 512],
                         start=(ic == 0), stop=(ic == 7),
                         reads=(ow16, attnout), tag="omm")
                xr = xres[oc % 2]
                p.dma("sync", xr.ap, xo[oc * 128:(oc + 1) * 128, q0:q0 + 512],
                      writes=(xr,), tag="xres")
                p.ts("dve", tmp1, tmp1.ap, bP.ap,
                     gb2.ap[:, oc:oc + 1], ALU.add, reads=(bP, gb2), tag="obias")
                p.tt("dve", xpre, xpre.ap[:, oc * 512:(oc + 1) * 512],
                     tmp1.ap, xr.ap, ALU.add, reads=(tmp1, xr), tag="ores")

            def xpre_ap(c, _qc=qc):
                return xpre.ap[:, c * 512:(c + 1) * 512]

            def store(cc, oap, o32t, _q0=q0):
                p.dma("sync", xout[cc * 128:(cc + 1) * 128, _q0:_q0 + 512],
                      oap, reads=(o32t,), writes=(bld.dram_tiles["x_out"],),
                      tag="xout")
            bld.emit_ln(xpre, xpre_ap, store, lns)

    p.op("sync", lambda eng: eng.nop(),
         reads=(bld.dram_tiles["x_out"],), tag="final_wait")


def emit_l1a(bld):
    emit_mha(bld, {"gated": True, "beta": 1.0 / 32.0, "masked": False,
                   "causal": False})


def emit_l2a(bld):
    emit_mha(bld, {"gated": False, "beta": 0.125, "masked": True,
                   "causal": True})


def _spec_l1a():
    return (emit_l1a, [
        ("qkvT", "l1a_qkvT", [D, 3 * D], F32, False),
        ("qkvmT", "l1a_qkvmT", [D, 3 * D], F32, False),
        ("owT", "l1a_owT", [D, D], F32, False),
        ("owmT", "l1a_owmT", [D, D], F32, False),
        ("ob2", "l1a_ob", [128, 8], F32, False),
        ("obm2", "l1a_obm", [128, 8], F32, False),
    ], True)


def _spec_l2a():
    return (emit_l2a, [
        ("wq_in", "l2a_wq", [D, D], F32, False),
        ("wk_in", "l2a_wk", [D, D], F32, False),
        ("wv_in", "l2a_wv", [D, D], F32, False),
        ("ow_in", "l2a_wo", [D, D], F32, False),
        ("ob2", "l2a_bo", [128, 8], F32, False),
        ("mask", "l2a_mask", [NQC, 32, 128, 512], BF16, True),
    ], True)




def emit_l2f(bld):
    nc, p = bld.nc, bld.p
    HD = 4096
    NH = HD // 128
    xo = bld.din("x_own", [D, TOWN], F32)
    w1T = bld.din("w1T", [D, HD], F32)
    w2T = bld.din("w2T", [D, HD], F32)
    h1T = bld.din("h1T", [HD, HD], F32)
    h2T = bld.din("h2T", [HD, HD], F32)
    w3T = bld.din("w3T", [HD, D], F32)
    w1s = bld.din("w1s", [128, 32], F32)
    w2s = bld.din("w2s", [128, 32], F32)
    h1s = bld.din("h1s", [128, 32], F32)
    h2s = bld.din("h2s", [128, 32], F32)
    w3s = bld.din("w3s", [128, 8], F32)
    xout = bld.dout("x_out", [D, TOWN], F32)

    with p.scope():
        s1t = p.sbuf("s1t", [128, 32], F32)
        s2t = p.sbuf("s2t", [128, 32], F32)
        sh1t = p.sbuf("sh1t", [128, 32], F32)
        sh2t = p.sbuf("sh2t", [128, 32], F32)
        s3t = p.sbuf("s3t", [128, 8], F32)
        for t_, ap in ((s1t, w1s), (s2t, w2s), (sh1t, h1s), (sh2t, h2s),
                       (s3t, w3s)):
            p.dma("sync", t_.ap, ap, writes=(t_,), tag="scl")
        x16 = p.sbuf("x16", [128, 8 * 512], BF16)
        h16 = p.sbuf("h16", [128, NH * 512], BF16)
        g16 = p.sbuf("g16", [128, NH * 512], BF16)
        wab = [p.sbuf(f"wab{i}", [128, 8 * 512], BF16) for i in range(2)]
        hbb = [p.sbuf(f"hbb{i}", [128, NH * 256], BF16) for i in range(2)]
        w3b = p.sbuf("w3b", [128, NH * 512], BF16)
        sil = [p.sbuf(f"sil{i}", [128, 512], F32) for i in range(2)]
        xres = [p.sbuf(f"xres{i}", [128, 512], F32) for i in range(2)]
        xpre = p.sbuf("xpre", [128, 8 * 512], F32)
        lns = bld.ln_scratch()

        def big_mm(dst16, dst_col, src16, src_cols, wb, wb_col, nin, bank,
                   act_fn, s_ap, reads):
            for ic in range(nin):
                p.mm(bank, bank.ap,
                     wb.ap[:, ic * (wb_col) : ic * wb_col + 128] if False else
                     wb.ap[:, ic * wb_col: ic * wb_col + 128],
                     src16.ap[:, src_cols[ic]:src_cols[ic] + 512],
                     start=(ic == 0), stop=(ic == nin - 1),
                     reads=(wb, src16), tag="bmm")
            p.act(dst16, dst16.ap[:, dst_col:dst_col + 512], bank.ap, act_fn,
                  scale=s_ap, reads=(bank,) + tuple(reads), tag="bact")

        for tc in range(4):
            t0 = tc * 512
            p.dma("gp", x16.ap.rearrange("p (c t) -> p c t", c=8),
                  xo.rearrange("(c p) t -> p c t", p=128)[:, :, t0:t0 + 512],
                  writes=(x16,), tag="x16")
            for s in range(8):
                wa, wb_ = wab[0], wab[1]
                p.dma("gp", wa.ap.rearrange("p (c e) -> p c e", c=8),
                      w1T.rearrange("(c p) e -> p c e", p=128)[
                          :, :, s * 512:(s + 1) * 512], writes=(wa,), tag="w1")
                p.dma("gp", wb_.ap.rearrange("p (c e) -> p c e", c=8),
                      w2T.rearrange("(c p) e -> p c e", p=128)[
                          :, :, s * 512:(s + 1) * 512], writes=(wb_,), tag="w2")
                for fl in range(4):
                    fc = s * 4 + fl
                    b1, b2 = bld.banks[0], bld.banks[1]
                    for dd in range(8):
                        p.mm(b1, b1.ap,
                             wa.ap[:, dd * 512 + fl * 128: dd * 512 + (fl + 1) * 128],
                             x16.ap[:, dd * 512:(dd + 1) * 512],
                             start=(dd == 0), stop=(dd == 7), reads=(wa, x16),
                             tag="amm1")
                    for dd in range(8):
                        p.mm(b2, b2.ap,
                             wb_.ap[:, dd * 512 + fl * 128: dd * 512 + (fl + 1) * 128],
                             x16.ap[:, dd * 512:(dd + 1) * 512],
                             start=(dd == 0), stop=(dd == 7), reads=(wb_, x16),
                             tag="amm2")
                    sl = sil[fl % 2]
                    p.act(sl, sl.ap, b1.ap, AF.Silu,
                          scale=s1t.ap[:, fc:fc + 1], reads=(b1, s1t),
                          tag="asilu")
                    p.op("dve", lambda eng, _b2=b2, _sl=sl, _fc=fc: eng.scalar_tensor_tensor(
                        out=h16.ap[:, _fc * 512:(_fc + 1) * 512], in0=_b2.ap,
                        scalar=s2t.ap[:, _fc:_fc + 1], in1=_sl.ap,
                        op0=ALU.mult, op1=ALU.mult),
                        reads=(b2, sl, s2t), writes=(h16,), tag="ah")
            for s in range(16):
                hb = hbb[s % 2]
                p.dma("gp", hb.ap.rearrange("p (c e) -> p c e", c=NH),
                      h1T.rearrange("(c p) e -> p c e", p=128)[
                          :, :, s * 256:(s + 1) * 256], writes=(hb,), tag="h1")
                for fl in range(2):
                    fc = s * 2 + fl
                    bk = bld.banks[fl % 2]
                    for ic in range(NH):
                        p.mm(bk, bk.ap,
                             hb.ap[:, ic * 256 + fl * 128: ic * 256 + (fl + 1) * 128],
                             h16.ap[:, ic * 512:(ic + 1) * 512],
                             start=(ic == 0), stop=(ic == NH - 1),
                             reads=(hb, h16), tag="bmm1")
                    p.act(g16, g16.ap[:, fc * 512:(fc + 1) * 512], bk.ap,
                          AF.Gelu, scale=sh1t.ap[:, fc:fc + 1],
                          reads=(bk, sh1t), tag="bgelu")
            for s in range(16):
                hb = hbb[s % 2]
                p.dma("gp", hb.ap.rearrange("p (c e) -> p c e", c=NH),
                      h2T.rearrange("(c p) e -> p c e", p=128)[
                          :, :, s * 256:(s + 1) * 256], writes=(hb,), tag="h2")
                for fl in range(2):
                    fc = s * 2 + fl
                    bk = bld.banks[fl % 2]
                    for ic in range(NH):
                        p.mm(bk, bk.ap,
                             hb.ap[:, ic * 256 + fl * 128: ic * 256 + (fl + 1) * 128],
                             g16.ap[:, ic * 512:(ic + 1) * 512],
                             start=(ic == 0), stop=(ic == NH - 1),
                             reads=(hb, g16), tag="cmm")
                    p.act(h16, h16.ap[:, fc * 512:(fc + 1) * 512], bk.ap,
                          AF.Gelu, scale=sh2t.ap[:, fc:fc + 1],
                          reads=(bk, sh2t), tag="cgelu")
            for ws in range(2):
                p.dma("gp", w3b.ap.rearrange("p (c e) -> p c e", c=NH),
                      w3T.rearrange("(c p) e -> p c e", p=128)[
                          :, :, ws * 512:(ws + 1) * 512],
                      writes=(w3b,), tag="w3")
                for ol in range(4):
                    oc = ws * 4 + ol
                    bk = bld.banks[ol % 2]
                    for ic in range(NH):
                        p.mm(bk, bk.ap,
                             w3b.ap[:, ic * 512 + ol * 128: ic * 512 + (ol + 1) * 128],
                             h16.ap[:, ic * 512:(ic + 1) * 512],
                             start=(ic == 0), stop=(ic == NH - 1),
                             reads=(w3b, h16), tag="dmm")
                    xr = xres[oc % 2]
                    p.dma("sync", xr.ap, xo[oc * 128:(oc + 1) * 128, t0:t0 + 512],
                          writes=(xr,), tag="xres")
                    p.op("dve", lambda eng, _bk=bk, _xr=xr, _oc=oc: eng.scalar_tensor_tensor(
                        out=xpre.ap[:, _oc * 512:(_oc + 1) * 512], in0=_bk.ap,
                        scalar=s3t.ap[:, _oc:_oc + 1], in1=_xr.ap,
                        op0=ALU.mult, op1=ALU.add),
                        reads=(bk, xr, s3t), writes=(xpre,), tag="dres")

            def xpre_ap(c, _tc=tc):
                return xpre.ap[:, c * 512:(c + 1) * 512]

            def store(cc, oap, o32t, _t0=t0):
                p.dma("sync", xout[cc * 128:(cc + 1) * 128, _t0:_t0 + 512],
                      oap, reads=(o32t,), writes=(bld.dram_tiles["x_out"],),
                      tag="xout")
            bld.emit_ln(xpre, xpre_ap, store, lns)

    p.op("sync", lambda eng: eng.nop(),
         reads=(bld.dram_tiles["x_out"],), tag="final_wait")


def _spec_l2f():
    return (emit_l2f, [
        ("w1T", "l2f_w1T", [D, 4096], F32, False),
        ("w2T", "l2f_w2T", [D, 4096], F32, False),
        ("h1T", "l2f_h1T", [4096, 4096], F32, False),
        ("h2T", "l2f_h2T", [4096, 4096], F32, False),
        ("w3T", "l2f_w3T", [4096, D], F32, False),
        ("w1s", "l2f_w1s", [128, 32], F32, False),
        ("w2s", "l2f_w2s", [128, 32], F32, False),
        ("h1s", "l2f_h1s", [128, 32], F32, False),
        ("h2s", "l2f_h2s", [128, 32], F32, False),
        ("w3s", "l2f_w3s", [128, 8], F32, False),
    ], False)




def emit_l1f(bld):
    nc, p = bld.nc, bld.p
    E = 16
    xo = bld.din("x_own", [D, TOWN], F32)
    routerT = bld.din("routerT", [D, 16], F32)
    rbias = bld.din("rbias", [1, 16], F32)
    gate_w = bld.din("gate_w", [E, D, 1024], F32)
    up_w = bld.din("up_w", [E, D, 1024], F32)
    down_w = bld.din("down_w", [E, 1024, D], F32)
    shg = bld.din("shg", [D, 2048], F32)
    shu = bld.din("shu", [D, 2048], F32)
    shd = bld.din("shd", [2048, D], F32)
    iota16 = bld.din("iota16", [1, 16], F32)
    eye128 = bld.din("eye128", [128, 128], BF16)
    xout = bld.dout("x_out", [D, TOWN], F32)

    with p.scope():
        xacc = p.sbuf("xacc", [128, 8 * TOWN], F32)
        x16 = p.sbuf("x16", [128, 8 * TOWN], BF16)
        p.dma("sync", xacc.ap.rearrange("p (c t) -> p c t", c=8),
              xo.rearrange("(c p) t -> p c t", p=128), writes=(xacc,), tag="xa")
        p.dma("gp", x16.ap.rearrange("p (c t) -> p c t", c=8),
              xo.rearrange("(c p) t -> p c t", p=128), writes=(x16,), tag="x16")
        router32 = p.sbuf("router32", [128, 8 * 16], F32)
        p.dma("sync", router32.ap.rearrange("p (c e) -> p c e", c=8),
              routerT.rearrange("(c p) e -> p c e", p=128), writes=(router32,),
              tag="rw")
        eyet = p.sbuf("eyet", [128, 128], BF16)
        p.dma("sync", eyet.ap, eye128, writes=(eyet,), tag="eye")

        wrows = p.sbuf("wrows", [16, 2048], BF16)
        wrow0 = p.sbuf("wrow0", [1, 2048], BF16)
        wb16 = p.sbuf("wb16", [128, TOWN], BF16)
        p.routing_scope = p.scope()
        p.routing_scope.__enter__()
        sc = p.sbuf("sc", [128, 256], F32)
        sa = p.sbuf("sa", [128, 256], F32)
        tA = p.sbuf("tA", [128, 256], F32)
        tB = p.sbuf("tB", [128, 256], F32)
        msk = [p.sbuf(f"msk{i}", [128, 256], F32) for i in range(2)]
        oh = [p.sbuf(f"oh{i}", [128, 256], F32) for i in range(4)]
        g2 = p.sbuf("g2", [128, 64], F32)
        sec = p.sbuf("sec", [128, 64], F32)
        t64a = p.sbuf("t64a", [128, 64], F32)
        t64b = p.sbuf("t64b", [128, 64], F32)
        gsel = p.sbuf("gsel", [128, 64], F32)
        cnt = p.sbuf("cnt", [128, 16], F32)
        t16 = p.sbuf("t16", [128, 16], F32)
        mx = p.sbuf("mx", [128, 16], F32)
        idx = p.sbuf("idx", [128, 16], F32)
        twk = [p.sbuf(f"twk{i}", [128, 16], F32) for i in range(4)]
        tws = [p.sbuf(f"tws{i}", [128, 16], F32) for i in range(2)]
        fac = p.sbuf("fac", [128, 16], F32)
        rbb = p.sbuf("rbb", [128, 16], F32)
        iob = p.sbuf("iob", [128, 16], F32)
        rb1 = p.sbuf("rb1", [1, 16], BF16)
        io1 = p.sbuf("io1", [1, 16], BF16)
        rbf = p.sbuf("rbf", [1, 16], F32)
        iof = p.sbuf("iof", [1, 16], F32)
        wall = p.sbuf("wall", [128, 256], F32)
        w16t = p.sbuf("w16t", [128, 256], BF16)

        p.dma("sync", rbf.ap, rbias, writes=(rbf,), tag="rb")
        p.dma("sync", iof.ap, iota16, writes=(iof,), tag="io")
        p.cp("dve", rb1, rb1.ap, rbf.ap, reads=(rbf,), tag="rb1")
        p.cp("dve", io1, io1.ap, iof.ap, reads=(iof,), tag="io1")
        bR = bld.banks[2]
        p.mm(bR, bR.ap[:, 0:16], bld.ones1.ap, rb1.ap, start=True, stop=True,
             reads=(bld.ones1, rb1), tag="rbb_mm")
        p.cp("dve", rbb, rbb.ap, bR.ap[:, 0:16], reads=(bR,), tag="rbb_cp")
        p.mm(bR, bR.ap[:, 16:32], bld.ones1.ap, io1.ap, start=True, stop=True,
             reads=(bld.ones1, io1), tag="iob_mm")
        p.cp("dve", iob, iob.ap, bR.ap[:, 16:32], reads=(bR,), tag="iob_cp")

        for u in range(16):
            bL = bld.banks[u % 2]
            for dd in range(8):
                p.mm(bL, bL.ap[:, 0:16],
                     xacc.ap[:, dd * TOWN + u * 128: dd * TOWN + (u + 1) * 128],
                     router32.ap[:, dd * 16:(dd + 1) * 16],
                     start=(dd == 0), stop=(dd == 7), reads=(xacc, router32),
                     tag="lmm")
            p.act(sc, sc.ap[:, u * 16:(u + 1) * 16], bL.ap[:, 0:16],
                  AF.Sigmoid, reads=(bL,), tag="lsig")
        v3 = lambda t: t.ap.rearrange("p (u e) -> p u e", u=16)
        v4 = lambda t: t.ap.rearrange("p (u g j) -> p u g j", u=16, g=4)
        vg = lambda t: t.ap.rearrange("p (u g) -> p u g", u=16)
        p.tt("dve", sa, v3(sa), v3(sc),
             rbb.ap.unsqueeze(1).broadcast_to([128, 16, 16]), ALU.add,
             reads=(sc, rbb), tag="sbias")
        c_ = lambda j: v4(sa)[:, :, :, j]
        p.tt("dve", t64a, t64a.ap, c_(0), c_(1), ALU.max, reads=(sa,), tag="gf1")
        p.tt("dve", t64b, t64b.ap, c_(2), c_(3), ALU.max, reads=(sa,), tag="gf2")
        p.tt("dve", g2, g2.ap, t64a.ap, t64b.ap, ALU.max,
             reads=(t64a, t64b), tag="gf3")
        pairs = [(0, 1), (0, 2), (0, 3), (1, 2), (1, 3), (2, 3)]
        for i, (a_, b_) in enumerate(pairs):
            p.tt("dve", t64a, t64a.ap, c_(a_), c_(b_), ALU.min, reads=(sa,),
                 tag="gp1")
            if i == 0:
                p.cp("dve", sec, sec.ap, t64a.ap, reads=(t64a,), tag="gp2")
            else:
                p.tt("dve", sec, sec.ap, sec.ap, t64a.ap, ALU.max,
                     reads=(t64a,), tag="gp3")
        p.tt("dve", g2, g2.ap, g2.ap, sec.ap, ALU.add, reads=(sec,), tag="g2s")
        gc = lambda g: vg(g2)[:, :, g]
        for g in range(4):
            first = True
            for j in range(4):
                if j == g:
                    continue
                opx = ALU.is_gt if True else None
                p.tt("dve", t16, t16.ap, gc(j), gc(g), ALU.is_gt, reads=(g2,),
                     tag="rk1")
                if first:
                    p.cp("dve", cnt, cnt.ap, t16.ap, reads=(t16,), tag="rk2")
                    first = False
                else:
                    p.tt("dve", cnt, cnt.ap, cnt.ap, t16.ap, ALU.add,
                         reads=(t16,), tag="rk3")
                if j < g:
                    p.tt("dve", t16, t16.ap, gc(j), gc(g), ALU.is_equal,
                         reads=(g2,), tag="rk4")
                    p.tt("dve", cnt, cnt.ap, cnt.ap, t16.ap, ALU.add,
                         reads=(t16,), tag="rk5")
            p.ts("dve", gsel, vg(gsel)[:, :, g], cnt.ap, 2.0, ALU.is_lt,
                 reads=(cnt,), tag="rk6")
        p.ts("dve", tA, tA.ap, sa.ap, 1.0, ALU.add, reads=(sa,), tag="me1")
        p.tt("dve", tB, v4(tB), v4(tA),
             vg(gsel).unsqueeze(3).broadcast_to([128, 16, 4, 4]), ALU.mult,
             reads=(tA, gsel), tag="me2")
        p.ts("dve", msk[0], msk[0].ap, tB.ap, 1.0, ALU.subtract, reads=(tB,),
             tag="me3")
        iob_rep = iob.ap.unsqueeze(1).broadcast_to([128, 16, 16])
        for k in range(4):
            mcur = msk[k % 2]
            mnxt = msk[(k + 1) % 2]
            p.red(mx, mx.ap, v3(mcur), AX.X, ALU.max, reads=(mcur,), tag="tk1")
            p.tt("dve", tA, v3(tA), v3(mcur),
                 mx.ap.unsqueeze(2).broadcast_to([128, 16, 16]), ALU.is_equal,
                 reads=(mcur, mx), tag="tk2")
            p.tt("dve", tB, v3(tB), v3(tA), iob_rep, ALU.mult,
                 reads=(tA, iob), tag="tk3")
            p.ts("dve", tA, tA.ap, tA.ap, -1.0e9, ALU.mult, 1.0e9, ALU.add,
                 tag="tk4")
            p.tt("dve", tB, tB.ap, tB.ap, tA.ap, ALU.add, reads=(tA,), tag="tk5")
            p.red(idx, idx.ap, v3(tB), AX.X, ALU.min, reads=(tB,), tag="tk6")
            p.tt("dve", oh[k], v3(oh[k]), iob_rep,
                 idx.ap.unsqueeze(2).broadcast_to([128, 16, 16]), ALU.is_equal,
                 reads=(iob, idx), tag="tk7")
            p.tt("dve", tA, tA.ap, oh[k].ap, sc.ap, ALU.mult,
                 reads=(oh[k], sc), tag="tk8")
            p.red(twk[k], twk[k].ap, v3(tA), AX.X, ALU.add, reads=(tA,),
                  tag="tk9")
            p.op("dve", lambda eng, _o=oh[k], _mc=mcur, _mn=mnxt:
                 eng.scalar_tensor_tensor(out=_mn.ap, in0=_o.ap, scalar=-100.0,
                                          in1=_mc.ap, op0=ALU.mult, op1=ALU.add),
                 reads=(oh[k], mcur), writes=(mnxt,), tag="tk10")
            if k == 0:
                p.cp("dve", tws[0], tws[0].ap, twk[0].ap, reads=(twk[0],),
                     tag="tk11")
            else:
                p.tt("dve", tws[k % 2], tws[k % 2].ap, tws[(k + 1) % 2].ap,
                     twk[k].ap, ALU.add, reads=(tws[(k + 1) % 2], twk[k]),
                     tag="tk12")
        p.ts("dve", t16, t16.ap, tws[1].ap, 1.0e-20, ALU.add,
             reads=(tws[1],), tag="nrm1")
        p.recip(fac, fac.ap, t16.ap, reads=(t16,), tag="nrm2")
        p.ts("dve", fac, fac.ap, fac.ap, 2.5, ALU.mult, tag="nrm3")
        for k in range(4):
            p.tt("dve", t16, t16.ap, twk[k].ap, fac.ap, ALU.mult,
                 reads=(twk[k], fac), tag="cb1")
            p.tt("dve", tA, v3(tA), v3(oh[k]),
                 t16.ap.unsqueeze(2).broadcast_to([128, 16, 16]), ALU.mult,
                 reads=(oh[k], t16), tag="cb2")
            if k == 0:
                p.cp("dve", wall, wall.ap, tA.ap, reads=(tA,), tag="cb3")
            else:
                p.tt("dve", wall, wall.ap, wall.ap, tA.ap, ALU.add,
                     reads=(tA,), tag="cb4")
        p.cp("dve", w16t, w16t.ap, wall.ap, reads=(wall,), tag="w16")
        for u in range(16):
            p.op("pe", lambda eng, _u=u: eng.transpose(
                bld.bank_bf.ap[0:16, 0:128], w16t.ap[:, _u * 16:(_u + 1) * 16],
                eyet.ap),
                reads=(w16t, eyet), writes=(bld.bank_bf,), tag="wtr")
            p.cp("dve", wrows, wrows.ap[:, u * 128:(u + 1) * 128],
                 bld.bank_bf.ap[0:16, 0:128], reads=(bld.bank_bf,), tag="wtr2")

        p.routing_scope.__exit__(None, None, None)
        h16 = p.sbuf("h16", [128, 8 * TOWN], BF16)
        gbuf = p.sbuf("gbuf", [128, 8 * 512], BF16)
        ubuf = p.sbuf("ubuf", [128, 8 * 512], BF16)
        dbuf = p.sbuf("dbuf", [128, 8 * 1024], BF16)
        sil = [p.sbuf(f"sil{i}", [128, 512], F32) for i in range(2)]
        swb = [p.sbuf(f"swb{i}", [128, 512], F32) for i in range(2)]
        for e in range(18):
            routed = e < 16
            if routed:
                p.dma("sync", wrow0.ap, wrows.ap[e:e + 1, :], reads=(wrows,),
                      writes=(wrow0,), tag="wrow")
                for qc in range(NQC):
                    bW = bld.banks[6]
                    p.mm(bW, bW.ap, bld.ones1.ap,
                         wrow0.ap[0:1, qc * 512:(qc + 1) * 512],
                         start=True, stop=True, reads=(bld.ones1, wrow0),
                         tag="wbmm")
                    p.cp("dve", wb16, wb16.ap[:, qc * 512:(qc + 1) * 512],
                         bW.ap, reads=(bW,), tag="wbcp")
            for half in range(2):
                if routed:
                    gsrc = gate_w[e][:, half * 512:(half + 1) * 512]
                    usrc = up_w[e][:, half * 512:(half + 1) * 512]
                else:
                    se = e - 16
                    gsrc = shg[:, se * 1024 + half * 512: se * 1024 + (half + 1) * 512]
                    usrc = shu[:, se * 1024 + half * 512: se * 1024 + (half + 1) * 512]
                p.dma("gp", gbuf.ap.rearrange("p (c f) -> p c f", c=8),
                      gsrc.rearrange("(c p) f -> p c f", p=128), writes=(gbuf,),
                      tag="gld")
                p.dma("gp", ubuf.ap.rearrange("p (c f) -> p c f", c=8),
                      usrc.rearrange("(c p) f -> p c f", p=128), writes=(ubuf,),
                      tag="uld")
                for fl in range(4):
                    fc = half * 4 + fl
                    for qc in range(NQC):
                        bG, bU = bld.banks[0], bld.banks[1]
                        for dd in range(8):
                            p.mm(bG, bG.ap,
                                 gbuf.ap[:, dd * 512 + fl * 128: dd * 512 + (fl + 1) * 128],
                                 x16.ap[:, dd * TOWN + qc * 512: dd * TOWN + (qc + 1) * 512],
                                 start=(dd == 0), stop=(dd == 7),
                                 reads=(gbuf, x16), tag="gmm")
                        for dd in range(8):
                            p.mm(bU, bU.ap,
                                 ubuf.ap[:, dd * 512 + fl * 128: dd * 512 + (fl + 1) * 128],
                                 x16.ap[:, dd * TOWN + qc * 512: dd * TOWN + (qc + 1) * 512],
                                 start=(dd == 0), stop=(dd == 7),
                                 reads=(ubuf, x16), tag="umm")
                        sl = sil[qc % 2]
                        p.act(sl, sl.ap, bG.ap, AF.Silu, reads=(bG,), tag="esilu")
                        hap = h16.ap[:, fc * TOWN + qc * 512: fc * TOWN + (qc + 1) * 512]
                        if routed:
                            sw = swb[qc % 2]
                            p.tt("gp", sw, sw.ap, sl.ap,
                                 wb16.ap[:, qc * 512:(qc + 1) * 512], ALU.mult,
                                 reads=(sl, wb16), tag="eswb")
                            p.tt("dve", h16, hap, sw.ap, bU.ap, ALU.mult,
                                 reads=(sw, bU), tag="eh")
                        else:
                            p.tt("dve", h16, hap, sl.ap, bU.ap, ALU.mult,
                                 reads=(sl, bU), tag="eh2")
            if routed:
                dsrc = down_w[e]
            else:
                se = e - 16
                dsrc = shd[se * 1024:(se + 1) * 1024, :]
            p.dma("gp", dbuf.ap.rearrange("p (c f) -> p c f", c=8),
                  dsrc.rearrange("(c p) f -> p c f", p=128), writes=(dbuf,),
                  tag="dld")
            for dd in range(8):
                for qc in range(NQC):
                    bD = bld.banks[2 + (qc % 2)]
                    for fc in range(8):
                        p.mm(bD, bD.ap,
                             dbuf.ap[:, fc * 1024 + dd * 128: fc * 1024 + (dd + 1) * 128],
                             h16.ap[:, fc * TOWN + qc * 512: fc * TOWN + (qc + 1) * 512],
                             start=(fc == 0), stop=(fc == 7),
                             reads=(dbuf, h16), tag="dmm")
                    xap = xacc.ap[:, dd * TOWN + qc * 512: dd * TOWN + (qc + 1) * 512]
                    p.tt("dve", xacc, xap, xap, bD.ap, ALU.add, reads=(bD,),
                         tag="dacc")

        lns = bld.ln_scratch()
        for qc in range(NQC):
            q0 = qc * 512

            def xpre_ap(c, _qc=qc):
                return xacc.ap[:, c * TOWN + _qc * 512: c * TOWN + (_qc + 1) * 512]

            def store(cc, oap, o32t, _q0=q0):
                p.dma("sync", xout[cc * 128:(cc + 1) * 128, _q0:_q0 + 512],
                      oap, reads=(o32t,), writes=(bld.dram_tiles["x_out"],),
                      tag="xout")
            bld.emit_ln(xacc, xpre_ap, store, lns)

    p.op("sync", lambda eng: eng.nop(),
         reads=(bld.dram_tiles["x_out"],), tag="final_wait")


def _spec_l1f():
    return (emit_l1f, [
        ("routerT", "l1f_routerT", [D, 16], F32, False),
        ("rbias", "l1f_rbias", [1, 16], F32, False),
        ("gate_w", "l1f_gate", [16, D, 1024], F32, False),
        ("up_w", "l1f_up", [16, D, 1024], F32, False),
        ("down_w", "l1f_down", [16, 1024, D], F32, False),
        ("shg", "l1f_shg", [D, 2048], F32, False),
        ("shu", "l1f_shu", [D, 2048], F32, False),
        ("shd", "l1f_shd", [2048, D], F32, False),
        ("iota16", "iota16", [1, 16], F32, False),
        ("eye128", "eye128", [128, 128], BF16, False),
    ], False)




def emit_l3a(bld):
    nc, p = bld.nc, bld.p
    H, DH, M = 16, 64, 64
    NCH = 16
    xo = bld.din("x_own", [D, TOWN], F32)
    xf = bld.din("x_full", [2, D, TOWN], F32)
    wqkvT = bld.din("wqkvT", [D, 3 * D], F32)
    bqkv2 = bld.din("bqkv2", [128, 24], F32)
    bvrow = bld.din("bvrow", [1, D], F32)
    woutT = bld.din("woutT", [D, D], F32)
    bout2 = bld.din("bout2", [128, 8], F32)
    proj = bld.din("proj", [H, DH, M], F32)
    tril = bld.din("tril", [128, 128], F32)
    gate_h = bld.din("gate_h", [1, 1], F32)
    xout = bld.dout("x_out", [D, TOWN], F32)
    qdram = bld.dscratch("qdram", [NCH, 128, 8 * 128], BF16)
    kdram_o = bld.dscratch("kdram_o", [NCH, 128, 8 * 128], BF16)
    vdram_o = bld.dscratch("vdram_o", [NCH, 128, D], BF16)
    kdram_b = bld.dscratch("kdram_b", [NCH, 128, 8 * 128], BF16)
    vdram_b = bld.dscratch("vdram_b", [NCH, 128, D], BF16)

    with p.scope():
        wt16 = p.sbuf("wt16", [128, 8 * 3 * D], BF16)
        p.dma("gp", wt16.ap.rearrange("p (c e) -> p c e", c=8),
              wqkvT.rearrange("(c p) e -> p c e", p=128), writes=(wt16,),
              tag="wqkv")
        bq2 = p.sbuf("bq2", [128, 24], F32)
        p.dma("sync", bq2.ap, bqkv2, writes=(bq2,), tag="bq")
        bvr = p.sbuf("bvr", [1, D], F32)
        p.dma("sync", bvr.ap, bvrow, writes=(bvr,), tag="bvr")
        bvr16 = p.sbuf("bvr16", [1, D], BF16)
        p.cp("dve", bvr16, bvr16.ap, bvr.ap, reads=(bvr,), tag="bvr16")
        bvb = p.sbuf("bvb", [128, D], F32)
        for hf in range(2):
            bB = bld.banks[6]
            p.mm(bB, bB.ap, bld.ones1.ap, bvr16.ap[0:1, hf * 512:(hf + 1) * 512],
                 start=True, stop=True, reads=(bld.ones1, bvr16), tag="bvb_mm")
            p.cp("dve", bvb, bvb.ap[:, hf * 512:(hf + 1) * 512], bB.ap,
                 reads=(bB,), tag="bvb_cp")
        xkb = [p.sbuf(f"xkb{i}", [128, 8 * 512], BF16) for i in range(2)]
        stg = [p.sbuf(f"stg{i}", [128, 8 * 512], BF16) for i in range(2)]
        vstg = [p.sbuf(f"vstg{i}", [128, D], BF16) for i in range(2)]

        def qkv_pass(src_fn, do_q, kd, vd, qd):
            for g in range(4):
                xk = xkb[g % 2]
                p.dma("gp", xk.ap.rearrange("p (c t) -> p c t", c=8),
                      src_fn(g), writes=(xk,), tag="xk")
                parts = ([("q", 0, qd), ("k", 1, kd)] if do_q else
                         [("k", 1, kd)])
                for nm, pi, dst in parts:
                    st_ = stg[pi % 2]
                    for ec in range(8):
                        bK = bld.banks[ec % 2]
                        for dd in range(8):
                            p.mm(bK, bK.ap,
                                 wt16.ap[:, dd * 3 * D + pi * D + ec * 128:
                                         dd * 3 * D + pi * D + (ec + 1) * 128],
                                 xk.ap[:, dd * 512:(dd + 1) * 512],
                                 start=(dd == 0), stop=(dd == 7),
                                 reads=(wt16, xk), tag="pmm")
                        p.ts("dve", st_, st_.ap[:, ec * 512:(ec + 1) * 512],
                             bK.ap, bq2.ap[:, pi * 8 + ec: pi * 8 + ec + 1],
                             ALU.add, reads=(bK, bq2), tag="pbias")
                    st4 = st_.ap.rearrange("p (c s k) -> p s c k", c=8, s=4)
                    for s_ in range(4):
                        p.dma("sync",
                              dst.ap[4 * g + s_].rearrange("p (c k) -> p c k", c=8),
                              st4[:, s_], reads=(st_,),
                              writes=(bld.dram_tiles[dst.name],), tag="pst")
                for s_ in range(4):
                    vst = vstg[s_ % 2]
                    for hf in range(2):
                        bV = bld.banks[2 + hf]
                        for dd in range(8):
                            p.mm(bV, bV.ap,
                                 xk.ap[:, dd * 512 + s_ * 128: dd * 512 + (s_ + 1) * 128],
                                 wt16.ap[:, dd * 3 * D + 2 * D + hf * 512:
                                         dd * 3 * D + 2 * D + (hf + 1) * 512],
                                 start=(dd == 0), stop=(dd == 7),
                                 reads=(xk, wt16), tag="vmm")
                        p.tt("dve", vst, vst.ap[:, hf * 512:(hf + 1) * 512],
                             bV.ap, bvb.ap[:, hf * 512:(hf + 1) * 512], ALU.add,
                             reads=(bV, bvb), tag="vbias")
                    p.dma("sync", vd.ap[4 * g + s_], vst.ap, reads=(vst,),
                          writes=(bld.dram_tiles[vd.name],), tag="vstd")

        if bld.inst_h != 0:
            qkv_pass(lambda g: xf[0].rearrange("(c p) t -> p c t", p=128)[
                :, :, g * 512:(g + 1) * 512], False, kdram_b, vdram_b, None)
        qkv_pass(lambda g: xo.rearrange("(c p) t -> p c t", p=128)[
            :, :, g * 512:(g + 1) * 512], True, kdram_o, vdram_o, qdram)

    attnout = p.sbuf("attnout", [128, 8 * TOWN], BF16)
    with p.scope():
        proj16 = p.sbuf("proj16", [128, H * M], BF16)
        p.dma("gp", proj16.ap.rearrange("d (h m) -> d h m", h=H)[0:64],
              proj.rearrange("h d m -> d h m"), writes=(proj16,), tag="proj")
        p.dma("gp", proj16.ap.rearrange("d (h m) -> d h m", h=H)[64:128],
              proj.rearrange("h d m -> d h m"), writes=(proj16,), tag="proj2")
        g64 = p.sbuf("g64", [64, 1], F32)
        gt = p.sbuf("gt", [1, 1], F32)
        gt16 = p.sbuf("gt16", [1, 1], BF16)
        p.dma("sync", gt.ap, gate_h, writes=(gt,), tag="gt")
        p.cp("dve", gt16, gt16.ap, gt.ap, reads=(gt,), tag="gt16")
        bB = bld.banks[6]
        p.mm(bB, bB.ap[0:64, 0:1], bld.ones1.ap[0:1, 0:64], gt16.ap,
             start=True, stop=True, reads=(bld.ones1, gt16), tag="g64_mm")
        p.cp("dve", g64, g64.ap, bB.ap[0:64, 0:1], reads=(bB,), tag="g64_cp")
        kvs = [p.sbuf(f"kvs{h}", [64, 65], F32) for h in range(H)]
        kvs16 = [p.sbuf(f"kvs16{h}", [64, 65], BF16) for h in range(H)]
        for h in range(H):
            p.memset("dve", kvs[h], kvs[h].ap, 0.0)
        kblk = [p.sbuf(f"kblk{i}", [128, 8 * 128], BF16) for i in range(2)]
        qblk = [p.sbuf(f"qblk{i}", [128, 8 * 128], BF16) for i in range(2)]
        vtl = [p.sbuf(f"vtl{i}", [128, D], BF16) for i in range(2)]
        vaug = [p.sbuf(f"vaug{i}", [128, H * 65], BF16) for i in range(2)]
        for i in range(2):
            p.memset("dve", vaug[i], vaug[i].ap, 1.0)
        kp16 = p.sbuf("kp16", [128, M], BF16)
        qp16 = p.sbuf("qp16", [64, 128], BF16)
        kpT16 = p.sbuf("kpT16", [64, 128], BF16)
        awm = p.sbuf("awm", [128, 128], BF16)
        trl = p.sbuf("trl", [128, 128], F32)
        p.dma("sync", trl.ap, tril, writes=(trl,), tag="trl")
        dn = p.sbuf("dn", [1, 128], F32)
        dninv = p.sbuf("dninv", [1, 128], F32)
        dninv16 = p.sbuf("dninv16", [1, 128], BF16)
        dnb = p.sbuf("dnb", [64, 128], BF16)

        def head_kv_update(ci, h, kb, va, bank_u):
            cch, coff = h // 2, 64 * (h % 2)
            bK = bank_u
            p.mm(bK, bK.ap[:, 0:M],
                 kb.ap[coff:coff + 64, cch * 128:(cch + 1) * 128],
                 proj16.ap[coff:coff + 64, h * M:(h + 1) * M],
                 start=True, stop=True, reads=(kb, proj16), tag="kpmm")
            p.act(kp16, kp16.ap, bK.ap[:, 0:M], AF.Relu, reads=(bK,),
                  tag="kprelu")
            bU = bld.banks[3]
            p.mm(bU, bU.ap[0:M, 0:65], kp16.ap,
                 va.ap[:, h * 65:(h + 1) * 65],
                 start=True, stop=True, reads=(kp16, va), tag="updmm")
            p.tt("dve", kvs[h], kvs[h].ap, kvs[h].ap, bU.ap[0:M, 0:65],
                 ALU.add, reads=(bU,), tag="updacc")

        if bld.inst_h != 0:
            for ci in range(NCH):
                kb = kblk[ci % 2]
                va = vaug[ci % 2]
                vt = vtl[ci % 2]
                p.dma("sync", kb.ap, kdram_b.ap[ci],
                      reads=(bld.dram_tiles["kdram_b"],), writes=(kb,),
                      tag="kbl")
                p.dma("sync", vt.ap, vdram_b.ap[ci],
                      reads=(bld.dram_tiles["vdram_b"],), writes=(vt,),
                      tag="vbl")
                p.cp("gp", va,
                     va.ap.rearrange("p (h e) -> p h e", h=H)[:, :, 0:64],
                     vt.ap.rearrange("p (h m) -> p h m", h=H), reads=(vt,),
                     tag="vaug")
                for h in range(H):
                    head_kv_update(ci, h, kb, va, bld.banks[2])
            if bld.inst_h is None:
                for h in range(H):
                    p.ts("dve", kvs[h], kvs[h].ap, kvs[h].ap, g64.ap[:, 0:1],
                         ALU.mult, reads=(g64,), tag="gatekv")

        for ci in range(NCH):
            kb = kblk[ci % 2]
            qb = qblk[ci % 2]
            va = vaug[ci % 2]
            vt = vtl[ci % 2]
            p.dma("sync", kb.ap, kdram_o.ap[ci],
                  reads=(bld.dram_tiles["kdram_o"],), writes=(kb,), tag="kol")
            p.dma("sync", qb.ap, qdram.ap[ci],
                  reads=(bld.dram_tiles["qdram"],), writes=(qb,), tag="qol")
            p.dma("sync", vt.ap, vdram_o.ap[ci],
                  reads=(bld.dram_tiles["vdram_o"],), writes=(vt,), tag="vol")
            p.cp("gp", va, va.ap.rearrange("p (h e) -> p h e", h=H)[:, :, 0:64],
                 vt.ap.rearrange("p (h m) -> p h m", h=H), reads=(vt,),
                 tag="vaug2")
            for h in range(H):
                cch, coff = h // 2, 64 * (h % 2)
                bQ = bld.banks[4]
                p.mm(bQ, bQ.ap[0:M, 0:128],
                     proj16.ap[coff:coff + 64, h * M:(h + 1) * M],
                     qb.ap[coff:coff + 64, cch * 128:(cch + 1) * 128],
                     start=True, stop=True, reads=(proj16, qb), tag="qpmm")
                p.act(qp16, qp16.ap, bQ.ap[0:M, 0:128], AF.Relu, reads=(bQ,),
                      tag="qprelu")
                bT = bld.banks[5]
                p.mm(bT, bT.ap[0:M, 0:128],
                     proj16.ap[coff:coff + 64, h * M:(h + 1) * M],
                     kb.ap[coff:coff + 64, cch * 128:(cch + 1) * 128],
                     start=True, stop=True, reads=(proj16, kb), tag="ktmm")
                p.act(kpT16, kpT16.ap, bT.ap[0:M, 0:128], AF.Relu, reads=(bT,),
                      tag="ktrelu")
                p.cp("dve", kvs16[h], kvs16[h].ap, kvs[h].ap, reads=(kvs[h],),
                     tag="kvcast")
                bC = bld.banks[0]
                p.mm(bC, bC.ap[0:65, 0:128], kvs16[h].ap, qp16.ap,
                     start=True, stop=False, reads=(kvs16[h], qp16), tag="ctx1")
                bA = bld.banks[1]
                p.mm(bA, bA.ap[:, 0:128], kpT16.ap, qp16.ap, start=True,
                     stop=True, reads=(kpT16, qp16), tag="awmm")
                p.tt("dve", awm, awm.ap, bA.ap[:, 0:128], trl.ap, ALU.mult,
                     reads=(bA, trl), tag="awmask")
                p.mm(bC, bC.ap[0:65, 0:128], va.ap[:, h * 65:(h + 1) * 65],
                     awm.ap, start=False, stop=True, reads=(va, awm), tag="ctx2")
                p.ts("dve", dn, dn.ap, bC.ap[64:65, 0:128], 1.0e-6, ALU.add,
                     reads=(bC,), tag="dn")
                p.recip(dninv, dninv.ap, dn.ap, reads=(dn,), tag="dninv")
                p.cp("dve", dninv16, dninv16.ap, dninv.ap, reads=(dninv,),
                     tag="dni16")
                bB2 = bld.banks[6]
                p.mm(bB2, bB2.ap[0:64, 0:128], bld.ones1.ap[0:1, 0:64],
                     dninv16.ap, start=True, stop=True,
                     reads=(bld.ones1, dninv16), tag="dnbmm")
                p.cp("dve", dnb, dnb.ap, bB2.ap[0:64, 0:128], reads=(bB2,),
                     tag="dnbcp")
                p.tt("dve", attnout,
                     attnout.ap[coff:coff + 64,
                                cch * TOWN + ci * 128: cch * TOWN + (ci + 1) * 128],
                     bC.ap[0:64, 0:128], dnb.ap, ALU.mult, reads=(bC, dnb),
                     tag="attno")
                head_kv_update(ci, h, kb, va, bld.banks[2])

    with p.scope():
        ow16 = p.sbuf("ow16", [128, 8 * D], BF16)
        p.dma("gp", ow16.ap.rearrange("p (c e) -> p c e", c=8),
              woutT.rearrange("(c p) e -> p c e", p=128), writes=(ow16,),
              tag="owl")
        bo2 = p.sbuf("bo2", [128, 8], F32)
        p.dma("sync", bo2.ap, bout2, writes=(bo2,), tag="bo")
        xres = [p.sbuf(f"xres{i}", [128, 512], F32) for i in range(2)]
        xpre = p.sbuf("xpre", [128, 8 * 512], F32)
        tmp1 = p.sbuf("tmp1", [128, 512], F32)
        lns = bld.ln_scratch()
        for qc in range(NQC):
            q0 = qc * 512
            for oc in range(8):
                bP = bld.banks[oc % 2]
                for ic in range(8):
                    p.mm(bP, bP.ap,
                         ow16.ap[:, ic * D + oc * 128: ic * D + (oc + 1) * 128],
                         attnout.ap[:, ic * TOWN + q0: ic * TOWN + q0 + 512],
                         start=(ic == 0), stop=(ic == 7),
                         reads=(ow16, attnout), tag="omm")
                xr = xres[oc % 2]
                p.dma("sync", xr.ap, xo[oc * 128:(oc + 1) * 128, q0:q0 + 512],
                      writes=(xr,), tag="xres")
                p.ts("dve", tmp1, tmp1.ap, bP.ap, bo2.ap[:, oc:oc + 1],
                     ALU.add, reads=(bP, bo2), tag="obias")
                p.tt("dve", xpre, xpre.ap[:, oc * 512:(oc + 1) * 512],
                     tmp1.ap, xr.ap, ALU.add, reads=(tmp1, xr), tag="ores")

            def xpre_ap(c, _qc=qc):
                return xpre.ap[:, c * 512:(c + 1) * 512]

            def store(cc, oap, o32t, _q0=q0):
                p.dma("sync", xout[cc * 128:(cc + 1) * 128, _q0:_q0 + 512],
                      oap, reads=(o32t,), writes=(bld.dram_tiles["x_out"],),
                      tag="xout")
            bld.emit_ln(xpre, xpre_ap, store, lns)

    p.op("sync", lambda eng: eng.nop(),
         reads=(bld.dram_tiles["x_out"],), tag="final_wait")


def _spec_l3a():
    return (emit_l3a, [
        ("wqkvT", "l3a_wqkvT", [D, 3 * D], F32, False),
        ("bqkv2", "l3a_bqkv", [128, 24], F32, False),
        ("bvrow", "l3a_bvrow", [1, D], F32, False),
        ("woutT", "l3a_woutT", [D, D], F32, False),
        ("bout2", "l3a_bout", [128, 8], F32, False),
        ("proj", "l3a_proj", [16, 64, 64], F32, False),
        ("tril", "l3a_tril", [128, 128], F32, False),
        ("gate_h", "gate_h", [1, 1], F32, True),
    ], True)




def emit_l3fA(bld, lidx):
    nc, p = bld.nc, bld.p
    H, DH = 8, 64
    INNER = 512
    BETA = 8.0
    xo = bld.din("x_own", [D, TOWN], F32)
    xf = bld.din("x_full", [2, D, TOWN], F32)
    wqkv = bld.din("wqkv", [D, 3 * INNER], F32)
    qs3 = bld.din("qs3", [64, 1], F32)
    ks3 = bld.din("ks3", [64, 1], F32)
    wo_in = bld.din("wo_in", [INNER, D], F32)
    csq = bld.din("csq", [64, TOWN], F32)
    ssq = bld.din("ssq", [64, TOWN], F32)
    csko = bld.din("csko", [64, TOWN], F32)
    ssko = bld.din("ssko", [64, TOWN], F32)
    cskpo = bld.din("cskpo", [64, TOWN], F32)
    sskpo = bld.din("sskpo", [64, TOWN], F32)
    cskp = bld.din("cskp", [64, 512], F32)
    sskp = bld.din("sskp", [64, 512], F32)
    maskd = bld.din("mask", [4, 8, 128, 512], BF16)
    gate_h = bld.din("gate_h", [1, 1], F32)
    xout = bld.dout("x_out", [D, TOWN], F32)

    qr16 = p.sbuf("qr16", [64, TOWN], BF16)
    qrh = p.sbuf("qrh", [64, H * TOWN], BF16)
    krh = p.sbuf("krh", [64, H * TOWN], BF16)
    krp = p.sbuf("krp", [64, H * (TOWN + 512)], BF16)
    vaug = p.sbuf("vaugA", [128, 20 * (H * 65 + 7)], BF16)
    VSTRIDE = H * 65 + 7
    g128 = p.sbuf("g128", [128, 1], F32)
    gt = p.sbuf("gt", [1, 1], F32)

    with p.scope():
        wt16 = p.sbuf("wt16", [128, 8 * 3 * INNER], BF16)
        p.dma("gp", wt16.ap.rearrange("p (c e) -> p c e", c=8),
              wqkv.rearrange("(c p) e -> p c e", p=128), writes=(wt16,),
              tag="wqkv")
        qst = p.sbuf("qst", [64, 1], F32)
        kst = p.sbuf("kst", [64, 1], F32)
        p.dma("sync", qst.ap, qs3, writes=(qst,), tag="qs")
        p.dma("sync", kst.ap, ks3, writes=(kst,), tag="ks")
        gt16 = p.sbuf("gt16", [1, 1], BF16)
        p.dma("sync", gt.ap, gate_h, writes=(gt,), tag="gt")
        p.cp("dve", gt16, gt16.ap, gt.ap, reads=(gt,), tag="gt16")
        bB = bld.banks[6]
        p.mm(bB, bB.ap[:, 0:1], bld.ones1.ap, gt16.ap, start=True, stop=True,
             reads=(bld.ones1, gt16), tag="g128mm")
        p.cp("dve", g128, g128.ap, bB.ap[:, 0:1], reads=(bB,), tag="g128cp")
        p.memset("dve", vaug, vaug.ap, 1.0)

        xs16 = [p.sbuf(f"xs16{i}", [128, 8 * 512], BF16) for i in range(2)]
        xbnd = p.sbuf("xbnd", [128, 4], F32)
        xbnd2 = p.sbuf("xbnd2", [128, 4], F32)
        xbnd16 = p.sbuf("xbnd16", [128, 4], BF16)
        qraw = p.sbuf("qraw", [64, 512], F32)
        qsq16 = p.sbuf("qsq16", [64, 512], BF16)
        ss1 = p.sbuf("ss1", [1, 512], F32)
        rn = p.sbuf("rn", [1, 512], F32)
        rn16 = p.sbuf("rn16", [1, 512], BF16)
        rnb = p.sbuf("rnb", [64, 512], F32)
        qn = p.sbuf("qn", [64, 512], F32)
        rot = p.sbuf("rot", [64, 512], F32)
        rt1 = p.sbuf("rt1", [64, 512], F32)
        rt2 = p.sbuf("rt2", [64, 512], F32)
        ctab = [p.sbuf(f"ctab{i}", [64, 512], F32) for i in range(2)]
        stab = [p.sbuf(f"stab{i}", [64, 512], F32) for i in range(2)]
        vst = [p.sbuf(f"vstA{i}", [128, INNER], BF16) for i in range(2)]

        def build_shifted(dst, tc, src, boundary_prev):
            t0 = tc * 512
            p.dma("gp", dst.ap.rearrange("p (c t) -> p c t", c=8)[:, 0:4],
                  src(0, 4, t0, t0 + 512), writes=(dst,), tag="shl")
            if t0 > 0 or boundary_prev == "direct":
                off = -1
                p.dma("gp", dst.ap.rearrange("p (c t) -> p c t", c=8)[:, 4:8],
                      src(4, 8, t0 - 1, t0 + 511), writes=(dst,), tag="shh")
            else:
                p.dma("gp",
                      dst.ap.rearrange("p (c t) -> p c t", c=8)[:, 4:8, 1:512],
                      src(4, 8, 0, 511), writes=(dst,), tag="shh2")
                p.dma("sync", xbnd.ap.rearrange("p (f o) -> p f o", o=1),
                      xf[0].rearrange("(c p) t -> p c t", p=128)[
                          :, 4:8, TOWN - 1:TOWN], writes=(xbnd,), tag="bnd",
                      allow_slow_non_contiguous=True)
                p.ts("dve", xbnd2, xbnd2.ap, xbnd.ap, g128.ap[:, 0:1],
                     ALU.mult, reads=(xbnd, g128), tag="bndg")
                p.cp("dve", xbnd16, xbnd16.ap, xbnd2.ap, reads=(xbnd2,),
                     tag="bnd16")
                p.cp("dve", dst,
                     dst.ap.rearrange("p (c t) -> p c t", c=8)[:, 4:8, 0:1],
                     xbnd16.ap.rearrange("p (f o) -> p f o", o=1),
                     reads=(xbnd16,), tag="bndcp")

        def qk_chain(part, tc, dst_t, dst_col, xsrc, ct, st_, scale_t):
            for h in range(H):
                ec, sub = h // 2, h % 2
                bP = bld.banks[4]
                for dd in range(8):
                    p.mm(bP, bP.ap,
                         wt16.ap[:, dd * 3 * INNER + part * INNER + ec * 128:
                                 dd * 3 * INNER + part * INNER + (ec + 1) * 128],
                         xsrc.ap[:, dd * 512:(dd + 1) * 512],
                         start=(dd == 0), stop=(dd == 7), reads=(wt16, xsrc),
                         tag="qkmm")
                p.cp("dve", qraw, qraw.ap, bP.ap[64 * sub:64 * (sub + 1), :],
                     reads=(bP,), tag="qkev")
                p.act(qsq16, qsq16.ap, qraw.ap, AF.Square, reads=(qraw,),
                      tag="qk2")
                bN = bld.banks[5]
                p.mm(bN, bN.ap[0:1, :], bld.ones128.ap[0:64, 0:1], qsq16.ap,
                     start=True, stop=True, reads=(qsq16, bld.ones128),
                     tag="qknm")
                p.cp("dve", ss1, ss1.ap, bN.ap[0:1, :], reads=(bN,),
                     tag="qksd")
                p.ts("dve", ss1, ss1.ap, ss1.ap, 1.0e-24, ALU.max, tag="qkcl")
                p.rsqrt_pos(rn, rn.ap, ss1.ap, reads=(ss1,), tag="qkrn")
                p.cp("dve", rn16, rn16.ap, rn.ap, reads=(rn,), tag="qkrn16")
                bB2 = bld.banks[5]
                p.mm(bB2, bB2.ap[0:64, :], bld.ones1.ap[0:1, 0:64], rn16.ap,
                     start=True, stop=True, reads=(bld.ones1, rn16), tag="qkbc")
                p.cp("dve", rnb, rnb.ap, bB2.ap[0:64, :], reads=(bB2,),
                     tag="qkbc2")
                p.op("dve", lambda eng, _s=scale_t: eng.scalar_tensor_tensor(
                    out=qn.ap, in0=qraw.ap, scalar=_s.ap[:, 0:1], in1=rnb.ap,
                    op0=ALU.mult, op1=ALU.mult),
                    reads=(qraw, rnb, scale_t), writes=(qn,), tag="qkn")
                p.ts("dve", rot, rot.ap[0:32, :], qn.ap[32:64, :], -1.0,
                     ALU.mult, reads=(qn,), tag="rp1")
                p.cp("dve", rot, rot.ap[32:64, :], qn.ap[0:32, :], reads=(qn,),
                     tag="rp2")
                p.tt("gp", rt1, rt1.ap, qn.ap, ct.ap, ALU.mult,
                     reads=(qn, ct), tag="rp3")
                p.tt("gp", rt2, rt2.ap, rot.ap, st_.ap, ALU.mult,
                     reads=(rot, st_), tag="rp4")
                p.tt("dve", dst_t, dst_t.ap[:, dst_col(h):dst_col(h) + 512],
                     rt1.ap, rt2.ap, ALU.add, reads=(rt1, rt2), tag="rp5")

        for tc in range(4):
            xs = xs16[tc % 2]
            build_shifted(xs, tc,
                          lambda c0, c1, a, b: xo.rearrange(
                              "(c p) t -> p c t", p=128)[:, c0:c1, a:b], None)
            p.dma("sync", ctab[0].ap, csq[:, tc * 512:(tc + 1) * 512],
                  writes=(ctab[0],), tag="tq1")
            p.dma("sync", stab[0].ap, ssq[:, tc * 512:(tc + 1) * 512],
                  writes=(stab[0],), tag="tq2")
            qk_chain(0, tc, qrh, lambda h, _tc=tc: h * TOWN + _tc * 512,
                     xs, ctab[0], stab[0], qst)
            p.dma("sync", ctab[1].ap, csko[:, tc * 512:(tc + 1) * 512],
                  writes=(ctab[1],), tag="tk1")
            p.dma("sync", stab[1].ap, ssko[:, tc * 512:(tc + 1) * 512],
                  writes=(stab[1],), tag="tk2")
            qk_chain(1, tc, krh, lambda h, _tc=tc: h * TOWN + _tc * 512,
                     xs, ctab[1], stab[1], kst)
            p.dma("sync", ctab[0].ap, cskpo[:, tc * 512:(tc + 1) * 512],
                  writes=(ctab[0],), tag="tk3")
            p.dma("sync", stab[0].ap, sskpo[:, tc * 512:(tc + 1) * 512],
                  writes=(stab[0],), tag="tk4")
            qk_chain(1, tc, krp, lambda h, _tc=tc: h * (TOWN + 512) + _tc * 512,
                     xs, ctab[0], stab[0], kst)
            for s_ in range(4):
                vv = vst[s_ % 2]
                bV = bld.banks[4]
                for dd in range(8):
                    p.mm(bV, bV.ap,
                         xs.ap[:, dd * 512 + s_ * 128: dd * 512 + (s_ + 1) * 128],
                         wt16.ap[:, dd * 3 * INNER + 2 * INNER:
                                 dd * 3 * INNER + 3 * INNER],
                         start=(dd == 0), stop=(dd == 7), reads=(xs, wt16),
                         tag="vmmA")
                p.cp("dve", vv, vv.ap, bV.ap, reads=(bV,), tag="vevA")
                ti = 4 * tc + s_
                p.cp("gp", vaug,
                     vaug.ap.rearrange("p (t s) -> p t s", s=VSTRIDE)[
                         :, ti].rearrange("p (h e) -> p h e", h=H, e=65)[
                         :, :, 0:64] if False else
                     vaug.ap[:, ti * VSTRIDE: ti * VSTRIDE + H * 65].rearrange(
                         "p (h e) -> p h e", e=65)[:, :, 0:64],
                     vv.ap.rearrange("p (h m) -> p h m", h=H), reads=(vv,),
                     tag="vaugA")
        xs = xs16[0]
        t0p = TOWN - 512
        p.dma("gp", xs.ap.rearrange("p (c t) -> p c t", c=8)[:, 0:4],
              xf[0].rearrange("(c p) t -> p c t", p=128)[:, 0:4, t0p:t0p + 512],
              writes=(xs,), tag="pshl")
        p.dma("gp", xs.ap.rearrange("p (c t) -> p c t", c=8)[:, 4:8],
              xf[0].rearrange("(c p) t -> p c t", p=128)[:, 4:8, t0p - 1:t0p + 511],
              writes=(xs,), tag="pshh")
        p.dma("sync", ctab[1].ap, cskp, writes=(ctab[1],), tag="tp1")
        p.dma("sync", stab[1].ap, sskp, writes=(stab[1],), tag="tp2")
        qk_chain(1, 4, krp, lambda h: h * (TOWN + 512) + TOWN,
                 xs, ctab[1], stab[1], kst)
        for s_ in range(4):
            vv = vst[s_ % 2]
            bV = bld.banks[4]
            for dd in range(8):
                p.mm(bV, bV.ap,
                     xs.ap[:, dd * 512 + s_ * 128: dd * 512 + (s_ + 1) * 128],
                     wt16.ap[:, dd * 3 * INNER + 2 * INNER:
                             dd * 3 * INNER + 3 * INNER],
                     start=(dd == 0), stop=(dd == 7), reads=(xs, wt16),
                     tag="vmmP")
            p.cp("dve", vv, vv.ap, bV.ap, reads=(bV,), tag="vevP")
            ti = 16 + s_
            p.cp("gp", vaug,
                 vaug.ap[:, ti * VSTRIDE: ti * VSTRIDE + H * 65].rearrange(
                     "p (h e) -> p h e", e=65)[:, :, 0:64],
                 vv.ap.rearrange("p (h m) -> p h m", h=H), reads=(vv,),
                 tag="vaugP")
    attnout = p.sbuf("attnoutA", [128, 4 * TOWN], BF16)
    with p.scope():
        kstg = [p.sbuf(f"kstg{i}", [65, 128], BF16) for i in range(2)]
        for i in range(2):
            p.memset("dve", kstg[i], kstg[i].ap[64:65, :], 1.0)
        qstg = p.sbuf("qstg", [65, 512], BF16)
        absq = p.sbuf("absq", [64, 512], BF16)
        mv1 = p.sbuf("mv1", [64, 1], F32)
        mv2 = p.sbuf("mv2", [64, 1], F32)
        mv16 = p.sbuf("mv16", [64, 1], BF16)
        crow1 = p.sbuf("crow1", [1, 512], F32)
        mtl = [p.sbuf(f"mtlB{i}", [128, 512], BF16) for i in range(2)]
        ptl = [p.sbuf(f"ptlB{i}", [128, 512], BF16) for i in range(3)]
        lg = p.sbuf("lgB", [1, 512], F32)
        linv = p.sbuf("linvB", [1, 512], F32)
        linv16 = p.sbuf("linv16B", [1, 512], BF16)
        lb16 = p.sbuf("lb16B", [64, 512], BF16)
        for w in range(4):
            for h in range(H):
                if w == 0:
                    prv_ap = krp.ap[:, h * (TOWN + 512) + TOWN:
                                    h * (TOWN + 512) + TOWN + 512]
                else:
                    prv_ap = krp.ap[:, h * (TOWN + 512) + (w - 1) * 512:
                                    h * (TOWN + 512) + w * 512]
                cur_ap = krh.ap[:, h * TOWN + w * 512: h * TOWN + (w + 1) * 512]
                p.op("dve", lambda eng, _a=prv_ap: eng.tensor_reduce(
                    out=mv1.ap, in_=_a, axis=AX.X, op=ALU.max,
                    apply_absolute_value=True),
                    reads=(krp,), writes=(mv1,), tag="mv1")
                p.op("dve", lambda eng, _a=cur_ap: eng.tensor_reduce(
                    out=mv2.ap, in_=_a, axis=AX.X, op=ALU.max,
                    apply_absolute_value=True),
                    reads=(krh,), writes=(mv2,), tag="mv2")
                p.tt("dve", mv16, mv16.ap, mv1.ap, mv2.ap, ALU.max,
                     reads=(mv1, mv2), tag="mv3")
                p.cp("dve", qstg, qstg.ap[0:64, :],
                     qrh.ap[:, h * TOWN + w * 512: h * TOWN + (w + 1) * 512],
                     reads=(qrh,), tag="qsg")
                p.act(absq, absq.ap, qstg.ap[0:64, :], AF.Abs, reads=(qstg,),
                      tag="qsg2")
                bN = bld.banks[5]
                p.mm(bN, bN.ap[0:1, :], mv16.ap, absq.ap,
                     start=True, stop=True, reads=(mv16, absq), tag="qsg3")
                p.ts("dve", crow1, crow1.ap, bN.ap[0:1, :], -1.0, ALU.mult,
                     reads=(bN,), tag="qsg4")
                p.cp("dve", qstg, qstg.ap[64:65, :], crow1.ap, reads=(crow1,),
                     tag="qsg5")
                bO = bld.banks[0]
                for kb in range(8):
                    ks_ = kstg[kb % 2]
                    if kb < 4:
                        if w == 0:
                            kc = h * (TOWN + 512) + TOWN + kb * 128
                        else:
                            kc = h * (TOWN + 512) + (w - 1) * 512 + kb * 128
                        p.cp("dve", ks_, ks_.ap[0:64, :],
                             krp.ap[:, kc:kc + 128], reads=(krp,), tag="kstgc")
                        vti = (16 + kb) if w == 0 else 4 * (w - 1) + kb
                    else:
                        kc = h * TOWN + w * 512 + (kb - 4) * 128
                        p.cp("dve", ks_, ks_.ap[0:64, :],
                             krh.ap[:, kc:kc + 128], reads=(krh,), tag="kstgc")
                        vti = 4 * w + (kb - 4)
                    mt = mtl[kb % 2]
                    p.dma("sync", mt.ap, maskd[w, kb], writes=(mt,), tag="mtlB")
                    bS = bld.banks[2 + kb % 2]
                    p.mm(bS, bS.ap, ks_.ap, qstg.ap, start=True, stop=True,
                         reads=(ks_, qstg), tag="smmB")
                    pt = ptl[kb % 3]
                    p.act(pt, pt.ap, bS.ap, AF.Exp, scale=BETA, reads=(bS,),
                          tag="pexpB")
                    p.tt("gp", pt, pt.ap, pt.ap, mt.ap, ALU.mult, reads=(mt,),
                         tag="pmskB")
                    p.mm(bO, bO.ap[0:65, :],
                         vaug.ap[:, vti * VSTRIDE + h * 65:
                                 vti * VSTRIDE + (h + 1) * 65],
                         pt.ap, start=(kb == 0), stop=(kb == 7),
                         reads=(vaug, pt), tag="pvB")
                p.ts("dve", lg, lg.ap, bO.ap[64:65, :], 1.0e-30, ALU.add,
                     reads=(bO,), tag="lgB")
                p.recip(linv, linv.ap, lg.ap, reads=(lg,), tag="lvB")
                p.cp("dve", linv16, linv16.ap, linv.ap, reads=(linv,),
                     tag="lv16B")
                bB2 = bld.banks[5]
                p.mm(bB2, bB2.ap[0:64, :], bld.ones1.ap[0:1, 0:64], linv16.ap,
                     start=True, stop=True, reads=(bld.ones1, linv16),
                     tag="lbcB")
                p.cp("dve", lb16, lb16.ap, bB2.ap[0:64, :], reads=(bB2,),
                     tag="lbc2B")
                cch, coff = h // 2, 64 * (h % 2)
                p.tt("dve", attnout,
                     attnout.ap[coff:coff + 64,
                                cch * TOWN + w * 512: cch * TOWN + (w + 1) * 512],
                     bO.ap[0:64, :], lb16.ap, ALU.mult, reads=(bO, lb16),
                     tag="onrmB")

    if DBG_L3F:
        for nm, t_ in (("dbg_qrh", qrh), ("dbg_krh", krh), ("dbg_krp", krp),
                       ("dbg_attn", attnout), ("dbg_vaug", vaug)):
            shp = [t_.ap.shape[0], t_.ap.shape[1]] if hasattr(t_.ap, "shape") \
                else None
            dt_ = BF16
            dout = bld.dout(nm, list(t_.ap.shape), dt_)
            p.dma("sync", dout.ap, t_.ap, reads=(t_,), writes=(dout,),
                  tag=nm)
            p.op("sync", lambda eng: eng.nop(), reads=(dout,), tag="dbgw")

    with p.scope():
        ow16 = p.sbuf("ow16A", [128, 4 * D], BF16)
        p.dma("gp", ow16.ap.rearrange("p (c e) -> p c e", c=4),
              wo_in.rearrange("(c p) e -> p c e", p=128), writes=(ow16,),
              tag="owA")
        xres = [p.sbuf(f"xresA{i}", [128, 512], F32) for i in range(2)]
        ores = [p.sbuf(f"oresA{i}", [128, 512], F32) for i in range(2)]
        for qc in range(NQC):
            q0 = qc * 512
            for oc in range(8):
                bP = bld.banks[oc % 2]
                for ic in range(4):
                    p.mm(bP, bP.ap,
                         ow16.ap[:, ic * D + oc * 128: ic * D + (oc + 1) * 128],
                         attnout.ap[:, ic * TOWN + q0: ic * TOWN + q0 + 512],
                         start=(ic == 0), stop=(ic == 3),
                         reads=(ow16, attnout), tag="ommA")
                xr = xres[oc % 2]
                p.dma("sync", xr.ap, xo[oc * 128:(oc + 1) * 128, q0:q0 + 512],
                      writes=(xr,), tag="xresA")
                orr = ores[oc % 2]
                p.tt("dve", orr, orr.ap, bP.ap, xr.ap, ALU.add,
                     reads=(bP, xr), tag="oresA")
                p.dma("sync", xout[oc * 128:(oc + 1) * 128, q0:q0 + 512],
                      orr.ap, reads=(orr,), writes=(bld.dram_tiles["x_out"],),
                      tag="xoutA")

    p.op("sync", lambda eng: eng.nop(),
         reads=(bld.dram_tiles["x_out"],), tag="final_wait")


def emit_l3fB(bld, lidx, final):
    nc, p = bld.nc, bld.p
    HP = 2816
    NHC = HP // 128
    xo = bld.din("x_own", [D, TOWN], F32)
    xf = bld.din("x_full", [2, D, TOWN], F32)
    w1a = bld.din("w1a", [D, HP], F32)
    w1g = bld.din("w1g", [D, HP], F32)
    b1a2 = bld.din("b1a2", [128, NHC], F32)
    b1g2 = bld.din("b1g2", [128, NHC], F32)
    w2p = bld.din("w2p", [HP, D], F32)
    b22 = bld.din("b22", [128, 8], F32)
    lng2 = bld.din("lng2", [128, 8], F32)
    lnb2 = bld.din("lnb2", [128, 8], F32)
    gate_h = bld.din("gate_h", [1, 1], F32)
    if final:
        xorig = bld.din("x_orig", [D, TOWN], F32)
    xout = bld.dout("x_out", [D, TOWN], F32)

    with p.scope():
        lngt = p.sbuf("lngt", [128, 8], F32)
        lnbt = p.sbuf("lnbt", [128, 8], F32)
        b1at = p.sbuf("b1at", [128, NHC], F32)
        b1gt = p.sbuf("b1gt", [128, NHC], F32)
        b2t = p.sbuf("b2t", [128, 8], F32)
        for t_, ap in ((lngt, lng2), (lnbt, lnb2), (b1at, b1a2), (b1gt, b1g2),
                       (b2t, b22)):
            p.dma("sync", t_.ap, ap, writes=(t_,), tag="bvec")
        gt = p.sbuf("gtB", [1, 1], F32)
        gt16 = p.sbuf("gt16B", [1, 1], BF16)
        g128 = p.sbuf("g128B", [128, 1], F32)
        p.dma("sync", gt.ap, gate_h, writes=(gt,), tag="gtB")
        p.cp("dve", gt16, gt16.ap, gt.ap, reads=(gt,), tag="gt16B")
        bB = bld.banks[6]
        p.mm(bB, bB.ap[:, 0:1], bld.ones1.ap, gt16.ap, start=True, stop=True,
             reads=(bld.ones1, gt16), tag="g128mmB")
        p.cp("dve", g128, g128.ap, bB.ap[:, 0:1], reads=(bB,), tag="g128cpB")

        hln = p.sbuf("hln", [128, 8 * TOWN], BF16)
        xsh = p.sbuf("xsh", [128, 8 * 512], F32)
        xbnd = p.sbuf("xbndB", [128, 4], F32)
        xbnd2 = p.sbuf("xbnd2B", [128, 4], F32)
        lns = bld.ln_scratch()
        for tc in range(4):
            t0 = tc * 512
            p.dma("sync", xsh.ap.rearrange("p (c t) -> p c t", c=8)[:, 0:4],
                  xo.rearrange("(c p) t -> p c t", p=128)[:, 0:4, t0:t0 + 512],
                  writes=(xsh,), tag="shlB")
            if t0 > 0:
                p.dma("sync",
                      xsh.ap.rearrange("p (c t) -> p c t", c=8)[:, 4:8],
                      xo.rearrange("(c p) t -> p c t", p=128)[
                          :, 4:8, t0 - 1:t0 + 511],
                      writes=(xsh,), tag="shhB")
            else:
                p.dma("sync",
                      xsh.ap.rearrange("p (c t) -> p c t", c=8)[:, 4:8, 1:512],
                      xo.rearrange("(c p) t -> p c t", p=128)[:, 4:8, 0:511],
                      writes=(xsh,), tag="shh2B")
                p.dma("sync", xbnd.ap.rearrange("p (f o) -> p f o", o=1),
                      xf[0].rearrange("(c p) t -> p c t", p=128)[
                          :, 4:8, TOWN - 1:TOWN], writes=(xbnd,), tag="bndB",
                      allow_slow_non_contiguous=True)
                p.ts("dve", xbnd2, xbnd2.ap, xbnd.ap, g128.ap[:, 0:1],
                     ALU.mult, reads=(xbnd, g128), tag="bndgB")
                p.cp("dve", xsh,
                     xsh.ap.rearrange("p (c t) -> p c t", c=8)[:, 4:8, 0:1],
                     xbnd2.ap.rearrange("p (f o) -> p f o", o=1),
                     reads=(xbnd2,), tag="bndcpB")

            def xpre_ap(c, _tc=tc):
                return xsh.ap[:, c * 512:(c + 1) * 512]

            def store(cc, oap, o32t, _t0=t0):
                p.cp("gp", hln,
                     hln.ap[:, cc * TOWN + _t0: cc * TOWN + _t0 + 512], oap,
                     reads=(o32t,), tag="hlncp")
            bld.emit_ln(xsh, xpre_ap, store, lns,
                        gb=lambda c: (lngt.ap[:, c:c + 1], lnbt.ap[:, c:c + 1]))

        a16 = p.sbuf("a16", [128, NHC * 512], BF16)
        g16 = p.sbuf("g16", [128, NHC * 512], BF16)
        wba = [p.sbuf(f"wba{i}", [128, 8 * 256], BF16) for i in range(2)]
        wbg = [p.sbuf(f"wbg{i}", [128, 8 * 256], BF16) for i in range(2)]
        w2b = [p.sbuf(f"w2b{i}", [128, NHC * 128], BF16) for i in range(2)]
        act16 = p.sbuf("act16", [128, NHC * 512], BF16)
        xres = [p.sbuf(f"xresB{i}", [128, 512], F32) for i in range(2)]
        xpre = p.sbuf("xpreB", [128, 8 * 512], F32)
        if final:
            xog = [p.sbuf(f"xog{i}", [128, 512], F32) for i in range(2)]
        for tc in range(4):
            t0 = tc * 512
            for s in range(11):
                wa_, wg_ = wba[s % 2], wbg[s % 2]
                p.dma("gp", wa_.ap.rearrange("p (c e) -> p c e", c=8),
                      w1a.rearrange("(c p) e -> p c e", p=128)[
                          :, :, s * 256:(s + 1) * 256], writes=(wa_,), tag="w1a")
                p.dma("gp", wg_.ap.rearrange("p (c e) -> p c e", c=8),
                      w1g.rearrange("(c p) e -> p c e", p=128)[
                          :, :, s * 256:(s + 1) * 256], writes=(wg_,), tag="w1g")
                for fl in range(2):
                    fc = s * 2 + fl
                    bA, bG = bld.banks[0], bld.banks[1]
                    for dd in range(8):
                        p.mm(bA, bA.ap,
                             wa_.ap[:, dd * 256 + fl * 128: dd * 256 + (fl + 1) * 128],
                             hln.ap[:, dd * TOWN + t0: dd * TOWN + t0 + 512],
                             start=(dd == 0), stop=(dd == 7), reads=(wa_, hln),
                             tag="fmma")
                    for dd in range(8):
                        p.mm(bG, bG.ap,
                             wg_.ap[:, dd * 256 + fl * 128: dd * 256 + (fl + 1) * 128],
                             hln.ap[:, dd * TOWN + t0: dd * TOWN + t0 + 512],
                             start=(dd == 0), stop=(dd == 7), reads=(wg_, hln),
                             tag="fmmg")
                    p.ts("dve", a16, a16.ap[:, fc * 512:(fc + 1) * 512],
                         bA.ap, b1at.ap[:, fc:fc + 1], ALU.add,
                         reads=(bA, b1at), tag="fa")
                    p.act(g16, g16.ap[:, fc * 512:(fc + 1) * 512], bG.ap,
                          AF.Gelu, bias=b1gt.ap[:, fc:fc + 1],
                          reads=(bG, b1gt), tag="fg")
                    p.tt("gp", act16, act16.ap[:, fc * 512:(fc + 1) * 512],
                         a16.ap[:, fc * 512:(fc + 1) * 512],
                         g16.ap[:, fc * 512:(fc + 1) * 512], ALU.mult,
                         reads=(a16, g16), tag="fh")
            for oc in range(8):
                w2_ = w2b[oc % 2]
                p.dma("gp", w2_.ap.rearrange("p (c e) -> p c e", c=NHC),
                      w2p.rearrange("(c p) e -> p c e", p=128)[
                          :, :, oc * 128:(oc + 1) * 128], writes=(w2_,),
                      tag="w2l")
                bk = bld.banks[2 + oc % 2]
                for ic in range(NHC):
                    p.mm(bk, bk.ap,
                         w2_.ap[:, ic * 128:(ic + 1) * 128],
                         act16.ap[:, ic * 512:(ic + 1) * 512],
                         start=(ic == 0), stop=(ic == NHC - 1),
                         reads=(w2_, act16), tag="fmm2")
                xr = xres[oc % 2]
                p.dma("sync", xr.ap, xo[oc * 128:(oc + 1) * 128, t0:t0 + 512],
                      writes=(xr,), tag="xresB")
                p.ts("dve", xpre, xpre.ap[:, oc * 512:(oc + 1) * 512],
                     bk.ap, b2t.ap[:, oc:oc + 1], ALU.add, reads=(bk, b2t),
                     tag="fb2")
                p.tt("dve", xpre, xpre.ap[:, oc * 512:(oc + 1) * 512],
                     xpre.ap[:, oc * 512:(oc + 1) * 512], xr.ap, ALU.add,
                     reads=(xr,), tag="fx1")
                if final:
                    xg = xog[oc % 2]
                    p.dma("sync", xg.ap,
                          xorig[oc * 128:(oc + 1) * 128, t0:t0 + 512],
                          writes=(xg,), tag="xorig")
                    p.tt("dve", xpre, xpre.ap[:, oc * 512:(oc + 1) * 512],
                         xpre.ap[:, oc * 512:(oc + 1) * 512], xg.ap, ALU.add,
                         reads=(xg,), tag="fxo")
                else:
                    p.dma("sync", xout[oc * 128:(oc + 1) * 128, t0:t0 + 512],
                          xpre.ap[:, oc * 512:(oc + 1) * 512], reads=(xpre,),
                          writes=(bld.dram_tiles["x_out"],), tag="xoutB")
            if final:
                def xpre_ap2(c, _tc=tc):
                    return xpre.ap[:, c * 512:(c + 1) * 512]

                def store2(cc, oap, o32t, _t0=t0):
                    p.dma("sync", xout[cc * 128:(cc + 1) * 128, _t0:_t0 + 512],
                          oap, reads=(o32t,),
                          writes=(bld.dram_tiles["x_out"],), tag="xoutF")
                bld.emit_ln(xpre, xpre_ap2, store2, lns)

    p.op("sync", lambda eng: eng.nop(),
         reads=(bld.dram_tiles["x_out"],), tag="final_wait")



def emit_l3f1a(bld):
    emit_l3fA(bld, 0)


def emit_l3f1b(bld):
    emit_l3fB(bld, 0, False)


def emit_l3f2a(bld):
    emit_l3fA(bld, 1)


def emit_l3f2b(bld):
    emit_l3fB(bld, 1, True)


def _spec_l3fA(l):
    return [
        ("wqkv", f"l3f_wqkv_{l}", [D, 3 * 512], F32, False),
        ("qs3", f"l3f_qs_{l}", [64, 1], F32, False),
        ("ks3", f"l3f_ks_{l}", [64, 1], F32, False),
        ("wo_in", f"l3f_wo_{l}", [512, D], F32, False),
        ("csq", "l3f_csq_q", [64, TOWN], F32, True),
        ("ssq", "l3f_ssq_q", [64, TOWN], F32, True),
        ("csko", "l3f_csk_own", [64, TOWN], F32, True),
        ("ssko", "l3f_ssk_own", [64, TOWN], F32, True),
        ("cskpo", "l3f_cskp_own", [64, TOWN], F32, True),
        ("sskpo", "l3f_sskp_own", [64, TOWN], F32, True),
        ("cskp", "l3f_csk_prev", [64, 512], F32, True),
        ("sskp", "l3f_ssk_prev", [64, 512], F32, True),
        ("mask", "l3f_mask", [4, 8, 128, 512], BF16, True),
        ("gate_h", "gate_h", [1, 1], F32, True),
    ]


def _spec_l3fB(l):
    return [
        ("w1a", f"l3f_w1a_{l}", [D, 2816], F32, False),
        ("w1g", f"l3f_w1g_{l}", [D, 2816], F32, False),
        ("b1a2", f"l3f_b1a_{l}", [128, 22], F32, False),
        ("b1g2", f"l3f_b1g_{l}", [128, 22], F32, False),
        ("w2p", f"l3f_w2_{l}", [2816, D], F32, False),
        ("b22", f"l3f_b2_{l}", [128, 8], F32, False),
        ("lng2", f"l3f_lng_{l}", [128, 8], F32, False),
        ("lnb2", f"l3f_lnb_{l}", [128, 8], F32, False),
        ("gate_h", "gate_h", [1, 1], F32, True),
    ]


def _spec_l3f1a():
    return (emit_l3f1a, _spec_l3fA(0), True)


def _spec_l3f1b():
    return (emit_l3f1b, _spec_l3fB(0), True)


def _spec_l3f2a():
    return (emit_l3f2a, _spec_l3fA(1), True)


def _spec_l3f2b():
    return (emit_l3f2b, _spec_l3fB(1) + [("x_orig", None, [D, TOWN], F32, None)],
            True)



def _spec_l0f():
    return (emit_l0f, [
        ("l0f_wgT", "l0f_wgT", [D, 3072], I32, False),
        ("l0f_wuT", "l0f_wuT", [D, 3072], I32, False),
        ("l0f_wdT", "l0f_wdT", [3072, D], I32, False),
    ], False)


BLOCK_SPECS = {
    "l0a": _spec_l0a,
    "l0f": _spec_l0f,
    "l1a": _spec_l1a,
    "l1f": _spec_l1f,
    "l3a": _spec_l3a,
    "l3f1a": _spec_l3f1a,
    "l3f1b": _spec_l3f1b,
    "l3f2a": _spec_l3f2a,
    "l3f2b": _spec_l3f2b,
    "l2a": _spec_l2a,
    "l2f": _spec_l2f,
}


_prog_cache = {}


def build_block(name):
    if name in _prog_cache:
        return _prog_cache[name]
    emit_fn, spec, needs_xfull = BLOCK_SPECS[name]()
    nc = bass.Bass("TRN2", target_bir_lowering=False, debug=False)
    p = Prog(nc)
    bld = Bld(nc, p)
    bld.block_spec = spec
    bld.needs_xfull = needs_xfull
    emit_fn(bld)
    p.finalize()
    _prog_cache[name] = (nc, spec, needs_xfull)
    return _prog_cache[name]


def run_chain(block_names, x_halves, shared, percore, extra_state=None):
    for name in block_names:
        nc, spec, needs_xfull = build_block(name)
        in_maps = []
        for c in range(NCORE):
            b, h = c // 2, c % 2
            m = {"x_own": x_halves[b][h]}
            if needs_xfull:
                m["x_full"] = np.stack([x_halves[b][0], x_halves[b][1]])
            for devname, hostkey, shape, dtype, pc in spec:
                if hostkey is None:
                    continue
                src = percore[c][hostkey] if pc else shared[hostkey]
                m[devname] = src
            if extra_state is not None:
                m.update(extra_state.get((name, c), {}))
            in_maps.append(m)
        res = run_bass_kernel_spmd(nc, in_maps, core_ids=list(range(NCORE)))
        x_halves = [[np.asarray(res.results[2 * b + h]["x_out"])
                     for h in range(2)] for b in range(B)]
    return x_halves





FUSED = True

FUSED_SEQ = [
    ("l0a", 0), ("l0f", 1),
    ("l1a", 0), ("l1f", 1),
    ("l2a", 0), ("l2f", 1),
    ("l3a", 2),
    ("l3f1a", 0), ("l3f1b", 1), ("l3f2a", 0), ("l3f2b", "ext"),
]

_fused_cache = {}


def build_fused():
    if "prog" in _fused_cache:
        return _fused_cache["prog"]
    nc = bass.Bass("TRN2", target_bir_lowering=False, debug=False)
    p = Prog(nc)
    bld = Bld(nc, p)
    xin = nc.dram_tensor("x_pair", [2, D, TOWN], F32, kind="ExternalInput").ap()
    xout_ap = nc.dram_tensor("x_out", [2, D, TOWN], F32,
                             kind="ExternalOutput").ap()
    xout_tiles = [p.dram_tile(xout_ap[h], f"x_out{h}") for h in range(2)]
    pb = []
    for i in range(3):
        a = nc.dram_tensor(f"pbuf{i}", [2, D, TOWN], F32, kind="Internal").ap()
        pb.append((a, [p.dram_tile(a[h], f"pbuf{i}_{h}") for h in range(2)]))

    cur_ap = xin
    for blk, outb in FUSED_SEQ:
        emit_fn, spec, needs_xfull = BLOCK_SPECS[blk]()
        for h in range(2):
            bld.prefix = f"{blk}_h{h}__"
            bld.inst_h = h
            if outb == "ext":
                out_tile = xout_tiles[h]
                out_half_ap = xout_ap[h]
            else:
                out_tile = pb[outb][1][h]
                out_half_ap = pb[outb][0][h]
            bld.fused_x = {
                "x_own": cur_ap[h],
                "x_full": cur_ap,
                "x_orig": pb[2][0][h],
                "x_out": out_tile,
            }
            out_tile.ap = out_half_ap
            with p.scope():
                emit_fn(bld)
        cur_ap = xout_ap if outb == "ext" else pb[outb][0]
    p.op("sync", lambda eng: eng.nop(),
         reads=(xout_tiles[0], xout_tiles[1]), tag="fused_final")
    p.finalize()
    _fused_cache["prog"] = nc
    return nc


def fused_in_maps(shared, percore, x_halves):
    maps = []
    for c in range(NCORE):
        b = c // 2
        m = {"x_pair": np.stack([x_halves[b][0], x_halves[b][1]])}
        for blk, _ in FUSED_SEQ:
            _, spec, _ = BLOCK_SPECS[blk]()
            for h in range(2):
                for devname, hostkey, shape, dtype, pc in spec:
                    if hostkey is None:
                        continue
                    m[f"{blk}_h{h}__{devname}"] = (
                        percore[2 * b + h][hostkey] if pc else shared[hostkey])
        maps.append(m)
    return maps


ALL_BLOCKS = ["l0a", "l0f", "l1a", "l1f", "l2a", "l2f", "l3a",
              "l3f1a", "l3f1b", "l3f2a", "l3f2b"]


def kernel(**inputs):
    shared, percore = host_prep(inputs)
    x = np.asarray(inputs["L0_attn_x"], np.float32)
    x_halves = [[_c(x[b, :TOWN].T), _c(x[b, TOWN:].T)] for b in range(B)]
    if FUSED:
        nc = build_fused()
        maps = fused_in_maps(shared, percore, x_halves)
        res = run_bass_kernel_spmd(nc, maps, core_ids=list(range(NCORE)))
        out = np.stack([
            np.concatenate([np.asarray(res.results[2 * b]["x_out"][0]),
                            np.asarray(res.results[2 * b]["x_out"][1])],
                           axis=1).T
            for b in range(B)])
        return _c(out.astype(np.float32))
    x_halves = run_chain(ALL_BLOCKS[:7], x_halves, shared, percore)
    extra = {}
    for c in range(NCORE):
        b, h = c // 2, c % 2
        extra[("l3f2b", c)] = {"x_orig": x_halves[b][h]}
    x_halves = run_chain(ALL_BLOCKS[7:], x_halves, shared, percore,
                         extra_state=extra)
    out = np.stack([np.concatenate([x_halves[b][0], x_halves[b][1]], axis=1).T
                    for b in range(B)])
    return _c(out.astype(np.float32))
